# version 01
# baseline (speedup 1.0000x reference)
import contextlib
import numpy as np
import ml_dtypes
import concourse.bass as bass
import concourse.mybir as mybir
from concourse.bass_utils import run_bass_kernel_spmd

F32 = mybir.dt.float32
BF16 = mybir.dt.bfloat16
AF = mybir.ActivationFunctionType
ALU = mybir.AluOpType
AX = mybir.AxisListType

ENGINES = ("sync", "scalar", "vector", "tensor", "gpsimd")
DBG_SKIP_HEADS = False
SAFE_BLOCKS = False
DBG_C_PHASES = 3
DBG_C_CUT = 9


class Prog:

    def __init__(self, nc, stack):
        self.nc = nc
        self.stack = stack
        self.sems = {}

    def sem(self, name):
        h = self.stack.enter_context(self.nc.semaphore(name))
        self.sems[name] = [h, 0]
        return name

    def val(self, name):
        return self.sems[name][1]

    def wait(self, eng, name, v=None):
        h, c = self.sems[name]
        t = c if v is None else v
        if t > 0:
            self.q[eng].append(("wait", h, t))

    EDGE_SEM = {"vector": "sdv", "scalar": "sac", "tensor": "spe", "gpsimd": "sio"}

    def op(self, eng, fname, *a, inc=None, edge=False, **kw):
        incs = []
        if inc is not None:
            if isinstance(inc, tuple):
                inc = [inc]
            for name, n in inc:
                rec = self.sems[name]
                rec[1] += n
                incs.append((rec[0], n))
        if edge:
            name = self.EDGE_SEM[eng]
            rec = self.sems[name]
            rec[1] += 1
            incs.append((rec[0], 1))
        self.q[eng].append(("op", fname, a, kw, incs))
        if edge:
            rec = self.sems[self.EDGE_SEM[eng]]
            self.q[eng].append(("wait", rec[0], rec[1]))

    def clear_sems_block(self):
        nums = [rec[0].num for rec in self.sems.values()]
        lo, hi = min(nums), max(nums) + 1
        with self.nc.Block("semclear") as blk:
            def body(eng):
                eng.dma_reset(range(lo, hi))
                eng.sem_clear(range(lo, hi))
            blk.gpsimd(body)

    @contextlib.contextmanager
    def block(self, name=None):
        self.q = {e: [] for e in ENGINES}
        yield
        qs = self.q
        self.q = None
        with self.nc.Block(name) as blk:
            for e in ENGINES:
                entries = qs[e]
                if not entries:
                    continue

                def body(eng, entries=entries):
                    for ent in entries:
                        if ent[0] == "wait":
                            eng.wait_ge(ent[1], ent[2])
                        else:
                            _, fname, a, kw, incs = ent
                            r = getattr(eng, fname)(*a, **kw)
                            for h, n in incs:
                                r = r.then_inc(h, n)
                getattr(blk, e)(body)


def make_consts_a():
    idn = np.eye(128, dtype=np.float32)
    c = {}
    c["cid_bf"] = idn.astype(ml_dtypes.bfloat16)
    c["cid_f32"] = idn.copy()
    masks = np.zeros((128, 4 * 512), dtype=np.float32)
    for r in range(4):
        ii = r * 128 + np.arange(128)[:, None]
        jj = np.arange(512)[None, :]
        masks[:, r * 512:(r + 1) * 512] = np.where(jj <= ii, 0.0, -1e30)
    c["cmasks"] = masks.astype(ml_dtypes.bfloat16)
    lio = np.zeros((2, 2048), dtype=np.float32)
    lio[0, :] = 1.0
    lio[1, :] = -np.arange(2048, dtype=np.float32)
    c["clio4"] = lio
    c["cones1_bf"] = np.ones((1, 128), dtype=ml_dtypes.bfloat16)
    c["cones1_f32"] = np.ones((1, 128), dtype=np.float32)
    c["ceps"] = np.full((128, 1), 1e-5, dtype=np.float32)
    start = 2.0 ** (-2.0 ** -(np.log2(16) - 3))
    slopes = start * start ** np.arange(16)
    c["cslopes"] = slopes.reshape(16, 1).astype(np.float32)
    return c


def inputs_a(x_shard, w):
    m = dict(make_consts_a())
    m["x"] = np.ascontiguousarray(x_shard, dtype=np.float32)
    m["wq"] = w["L0_attn_Wq"]
    m["wk"] = w["L0_attn_Wk"]
    m["wv"] = w["L0_attn_Wv"]
    m["wp"] = w["L0_attn_Wproj"]
    m["qrw"] = w["L0_attn_q_rms_w"].reshape(1, 64)
    m["krw"] = w["L0_attn_k_rms_w"].reshape(1, 64)
    m["omg"] = w["L0_attn_omega"].reshape(16, 1)
    m["u"] = w["L0_attn_u"].reshape(16, 64)
    return m


def build_a(NSEQ=2, NTILE=16, dbg=()):
    nc = bass.Bass("TRN2", target_bir_lowering=False)
    nc.detect_race_conditions = False
    st = contextlib.ExitStack()
    p = Prog(nc, st)

    T = NTILE * 128
    NT = NSEQ * T

    x = nc.dram_tensor("x", [NT, 1024], F32, kind="ExternalInput")
    wq_d = nc.dram_tensor("wq", [1024, 1024], F32, kind="ExternalInput")
    wk_d = nc.dram_tensor("wk", [1024, 1024], F32, kind="ExternalInput")
    wv_d = nc.dram_tensor("wv", [1024, 1024], F32, kind="ExternalInput")
    wp_d = nc.dram_tensor("wp", [1024, 1024], F32, kind="ExternalInput")
    qrw_d = nc.dram_tensor("qrw", [1, 64], F32, kind="ExternalInput")
    krw_d = nc.dram_tensor("krw", [1, 64], F32, kind="ExternalInput")
    omg_d = nc.dram_tensor("omg", [16, 1], F32, kind="ExternalInput")
    u_d = nc.dram_tensor("u", [16, 64], F32, kind="ExternalInput")
    id_bf_d = nc.dram_tensor("cid_bf", [128, 128], BF16, kind="ExternalInput")
    id_f32_d = nc.dram_tensor("cid_f32", [128, 128], F32, kind="ExternalInput")
    masks_d = nc.dram_tensor("cmasks", [128, 2048], BF16, kind="ExternalInput")
    lio4_d = nc.dram_tensor("clio4", [2, 2048], F32, kind="ExternalInput")
    ones1bf_d = nc.dram_tensor("cones1_bf", [1, 128], BF16, kind="ExternalInput")
    ones1f_d = nc.dram_tensor("cones1_f32", [1, 128], F32, kind="ExternalInput")
    eps_d = nc.dram_tensor("ceps", [128, 1], F32, kind="ExternalInput")
    slopes_d = nc.dram_tensor("cslopes", [16, 1], F32, kind="ExternalInput")
    y = nc.dram_tensor("y", [NT, 1024], F32, kind="ExternalOutput")
    pairs_dram = nc.dram_tensor("pairs_dram", [2, 16, T], F32)
    bounce2 = nc.dram_tensor("bounce2", [16, 64], BF16)

    sb = lambda name, shape, dt: st.enter_context(nc.sbuf_tensor(name, shape, dt))
    id_bf = sb("id_bf", [128, 128], BF16)
    id_f32 = sb("id_f32", [128, 128], F32)
    masks = sb("masks", [128, 2048], BF16)
    lio4 = sb("lio4", [2, 2048], F32)
    ones1bf = sb("ones1bf", [1, 128], BF16)
    ones1f = sb("ones1f", [1, 128], F32)
    epsc = sb("epsc", [128, 1], F32)

    wbuf = sb("wbuf", [128, 2 * 8192], BF16)
    KT = sb("KT", [128, 8 * T], BF16)
    Vtm = sb("Vtm", [128, NTILE * 1024], BF16)
    urep = sb("urep", [128, 1024], BF16)
    qwrep = sb("qwrep", [128, 64], F32)
    kwrep = sb("kwrep", [128, 64], F32)
    u16 = sb("u16", [16, 64], F32)
    udmp = sb("udmp", [16, 64], F32)
    u16bf = sb("u16bf", [16, 64], BF16)
    uss16 = sb("uss16", [16, 1], F32)
    unrm = sb("unrm", [16, 1], F32)
    urin = sb("urin", [16, 1], F32)
    urow_bf = sb("urow_bf", [1, 1024], BF16)
    wrow = sb("wrow", [1, 64], F32)
    om16 = sb("om16", [16, 1], F32)
    omsp = sb("omsp", [16, 1], F32)
    om8col = sb("om8col", [16, 1], F32)
    slopes16 = sb("slopes16", [16, 1], F32)

    xbuf = [sb(f"xbuf{i}", [128, 1024], F32) for i in range(2)]
    xln_bf = sb("xln_bf", [128, 1024], BF16)
    xlt = sb("xlt", [128, 1024], BF16)
    scr = sb("scr", [128, max(2048, T)], F32)
    sc = scr
    probs = sb("probs", [128, max(1024, T)], BF16)
    ptT = sb("ptT", [128, max(1024, T)], BF16)
    kqn = sb("kqn", [128, 1024], BF16)
    y_sb = sb("y_sb", [128, 1024], BF16)
    yT = sb("yT", [128, 1024], BF16)
    jW8 = sb("jW8", [16, T], F32)
    Wg = sb("Wg", [16, T], F32)
    pairsH = [sb(f"pairsH{i}", [2, T], F32) for i in range(4)]
    qs = [sb(f"qs{i}", [128, 128], BF16) for i in range(2)]
    ssum = sb("ssum", [128, 1], F32)
    ssq = sb("ssq", [128, 1], F32)
    smean = sb("smean", [128, 1], F32)
    srstd = sb("srstd", [128, 1], F32)
    msq = sb("msq", [128, 1], F32)
    hss = sb("hss", [128, 16], F32)
    hrstd = sb("hrstd", [128, 16], F32)
    glt = sb("glt", [128, 16], F32)
    gtmp = sb("gtmp", [128, 16], F32)
    gtmp2 = sb("gtmp2", [128, 16], F32)
    gtmp3 = sb("gtmp3", [128, 16], F32)
    nmax = sb("nmax", [128, 1], F32)
    rsum = sb("rsum", [128, 1], F32)
    rinv = sb("rinv", [128, 1], F32)
    uss = sb("uss", [1, 16], F32)
    udump = sb("udump", [1, 64], F32)

    psd = lambda name, shape, dt: st.enter_context(nc.psum_tensor(name, shape, dt))
    pA = psd("pA", [128, 1024], F32)
    pB = psd("pB", [128, 1024], F32)
    pC = psd("pC", [128, 512], F32)
    pD = psd("pD", [128, 512], F32)
    pT = psd("pT", [128, 2048], BF16)

    for s in ("sld", "sldg", "sio", "sdv", "sac", "spe", "sst", "sbar"):
        p.sem(s)
    p.clear_sems_block()

    def barrier():
        for e in ("scalar", "vector", "tensor", "gpsimd"):
            p.op(e, "drain", inc=("sbar", 1))
        p.op("sync", "nop", inc=("sbar", 1))
        tgt = p.val("sbar")
        for e in ("scalar", "vector", "tensor", "gpsimd", "sync"):
            p.wait(e, "sbar", tgt)

    with p.block("setup"):
        for dst, src in ((id_bf, id_bf_d), (id_f32, id_f32_d), (masks, masks_d),
                         (lio4, lio4_d), (ones1bf, ones1bf_d), (ones1f, ones1f_d),
                         (epsc, eps_d), (slopes16, slopes_d), (u16, u_d), (om16, omg_d)):
            p.op("sync", "dma_start", dst[:, :], src[:, :], inc=("sld", 16))
        for slot, wsrc in ((0, wk_d), (1, wv_d)):
            p.op("gpsimd", "dma_start",
                 wbuf[:, slot * 8192:(slot + 1) * 8192].rearrange("p (c n) -> p c n", c=8),
                 wsrc.ap().rearrange("(c p) n -> p c n", p=128), inc=("sldg", 16))
        nld = p.val("sld")
        p.wait("vector", "sld", nld)
        p.op("vector", "scalar_tensor_tensor", udmp[:, :], u16[:, :], 1.0, u16[:, :],
             ALU.bypass, ALU.mult, accum_out=uss16[:, :], inc=("sdv", 1))
        p.wait("scalar", "sdv", None)
        p.op("scalar", "activation", unrm[:, :], uss16[:, :], AF.Ln, edge=True)
        p.op("scalar", "activation", unrm[:, :], unrm[:, :], AF.Exp, scale=0.5,
             inc=("sac", 1))
        p.wait("vector", "sac", None)
        p.op("vector", "tensor_scalar", unrm[:, :], unrm[:, :], 1e-6, None, ALU.max,
             edge=True)
        p.op("vector", "reciprocal", urin[:, :], unrm[:, :], edge=True)
        p.op("vector", "tensor_scalar", u16[:, :], u16[:, :], urin[:, :], None, ALU.mult,
             edge=True)
        p.op("vector", "tensor_copy", u16bf[:, :], u16[:, :], inc=("sdv", 1))
        p.wait("sync", "sdv", None)
        p.op("sync", "dma_start", bounce2[:, :], u16bf[:, :], inc=("sld", 16))
        p.wait("sync", "sld", None)
        p.op("sync", "dma_start", urow_bf[:, :],
             bounce2.ap().rearrange("a b -> (a b)"), inc=("sld", 16))
        p.wait("scalar", "sld", None)
        p.op("scalar", "activation", omsp[:, :], om16[:, :], AF.Exp, edge=True)
        p.op("scalar", "activation", omsp[:, :], omsp[:, :], AF.Ln, bias=1.0,
             inc=("sac", 1))
        p.wait("vector", "sac", None)
        p.op("vector", "tensor_tensor", om8col[:, :], omsp[:, :], slopes16[:, :], ALU.mult,
             edge=True)
        p.op("vector", "tensor_scalar", om8col[:, :], om8col[:, :], 8.0, None, ALU.mult,
             inc=("sdv", 1))
        p.wait("tensor", "sld", None)
        p.wait("tensor", "sldg", None)
        p.op("tensor", "matmul", pA[:, 0:512], ones1bf[:, :], urow_bf[:, 0:512],
             start=True, stop=True)
        p.op("tensor", "matmul", pA[:, 512:1024], ones1bf[:, :], urow_bf[:, 512:1024],
             start=True, stop=True, inc=("spe", 1))
        p.wait("vector", "spe", None)
        p.op("vector", "tensor_copy", urep[:, 0:512], pA[:, 0:512])
        p.op("vector", "tensor_copy", urep[:, 512:1024], pA[:, 512:1024], inc=("sdv", 1))
        for dsrc, dst in ((qrw_d, qwrep), (krw_d, kwrep)):
            p.wait("sync", "spe", None)
            p.op("sync", "dma_start", wrow[:, :], dsrc[:, :], inc=("sld", 16))
            p.wait("tensor", "sld", None)
            p.wait("tensor", "sdv", None)
            p.op("tensor", "matmul", pA[:, 0:64], ones1f[:, :], wrow[:, :],
                 start=True, stop=True, inc=("spe", 1))
            p.wait("vector", "spe", None)
            p.op("vector", "tensor_copy", dst[:, :], pA[:, 0:64], inc=("sdv", 1))

    def emit_ln(xb, out_bf):
        p.op("scalar", "activation", xln_bf[:, :], xb[:, :], AF.Copy, bias=0.0,
             accum_out=ssum[:, :], inc=("sac", 1))
        p.op("vector", "scalar_tensor_tensor", scr[:, 0:1024], xb[:, :], 1.0, xb[:, :],
             ALU.bypass, ALU.mult, accum_out=ssq[:, :])
        p.wait("vector", "sac", None)
        p.op("vector", "tensor_scalar", smean[:, :], ssum[:, :], 1.0 / 1024, None, ALU.mult,
             edge=True)
        p.op("vector", "tensor_tensor", msq[:, :], smean[:, :], smean[:, :], ALU.mult)
        p.op("vector", "tensor_scalar", ssq[:, :], ssq[:, :], 1.0 / 1024, None, ALU.mult,
             edge=True)
        p.op("vector", "tensor_tensor", srstd[:, :], ssq[:, :], msq[:, :], ALU.subtract,
             inc=("sdv", 1))
        p.wait("scalar", "sdv", None)
        p.op("scalar", "activation", srstd[:, :], srstd[:, :], AF.Ln, bias=epsc[:, :],
             edge=True)
        p.op("scalar", "activation", srstd[:, :], srstd[:, :], AF.Exp, scale=-0.5,
             inc=("sac", 1))
        p.wait("vector", "sac", None)
        p.op("vector", "tensor_scalar", out_bf[:, :], xb[:, :], smean[:, :], srstd[:, :],
             ALU.subtract, ALU.mult, inc=("sdv", 1))

    def emit_transpose8(src_bf, dst):
        for half in range(2):
            p.wait("tensor", "sdv", None)
            for c in range(4):
                cc = half * 4 + c
                p.op("tensor", "transpose",
                     pT[:, half * 1024 + c * 128: half * 1024 + (c + 1) * 128],
                     src_bf[:, cc * 128:(cc + 1) * 128], id_bf[:, :],
                     inc=("spe", 1) if c == 3 else None)
            p.wait("vector", "spe", None)
            p.op("vector", "tensor_copy",
                 dst[:, half * 512:half * 512 + 512],
                 pT[:, half * 1024: half * 1024 + 512],
                 inc=("sdv", 1))

    def emit_proj(dst_ps, lhs, slot):
        for nh in range(2):
            for c in range(8):
                p.op("tensor", "matmul", dst_ps[:, nh * 512:(nh + 1) * 512],
                     lhs[:, c * 128:(c + 1) * 128],
                     wbuf[:, slot * 8192 + c * 1024 + nh * 512:
                          slot * 8192 + c * 1024 + (nh + 1) * 512],
                     start=(c == 0), stop=(c == 7),
                     inc=("spe", 1) if (c == 7) else None)

    def emit_rms(src_ps, wrep):
        p.wait("scalar", "spe", None)
        for h in range(16):
            p.op("scalar", "activation", scr[:, 1024 + h * 64:1024 + (h + 1) * 64],
                 src_ps[:, h * 64:(h + 1) * 64], AF.Square,
                 accum_out=hss[:, h:h + 1], inc=("sac", 1) if h == 15 else None)
        p.wait("vector", "sac", None)
        p.op("vector", "tensor_scalar", hrstd[:, :], hss[:, :], 1.0 / 64, None, ALU.mult,
             inc=("sdv", 1))
        p.wait("scalar", "sdv", None)
        p.op("scalar", "activation", hrstd[:, :], hrstd[:, :], AF.Ln, bias=epsc[:, :],
             edge=True)
        p.op("scalar", "activation", hrstd[:, :], hrstd[:, :], AF.Exp, scale=-0.5,
             inc=("sac", 1))
        p.wait("vector", "sac", None)
        for h in range(16):
            p.op("vector", "scalar_tensor_tensor", kqn[:, h * 64:(h + 1) * 64],
                 src_ps[:, h * 64:(h + 1) * 64], hrstd[:, h:h + 1], wrep[:, :],
                 ALU.mult, ALU.mult, inc=("sdv", 1) if h == 15 else None)

    pairs_ld = {}
    for s in range(NSEQ):
        with p.block(f"s{s}_p2"):
            if s > 0:
                p.wait("gpsimd", "spe", None)
                for slot, wsrc in ((0, wk_d), (1, wv_d)):
                    p.op("gpsimd", "dma_start",
                         wbuf[:, slot * 8192:(slot + 1) * 8192].rearrange("p (c n) -> p c n", c=8),
                         wsrc.ap().rearrange("(c p) n -> p c n", p=128), inc=("sldg", 16))
            wready = p.val("sldg")
            for t in range(NTILE):
                if SAFE_BLOCKS:
                    barrier()
                xb = xbuf[t % 2]
                p.wait("sync", "sdv", None)
                p.op("sync", "dma_start", xb[:, :],
                     x[s * T + t * 128: s * T + (t + 1) * 128, :], inc=("sld", 16))
                p.wait("scalar", "sld", None)
                p.wait("vector", "sld", None)
                emit_ln(xb, xln_bf)
                p.wait("tensor", "sdv", None)
                if t == 0:
                    p.wait("tensor", "sldg", wready)
                emit_transpose8(xln_bf, xlt)
                p.wait("tensor", "sdv", None)
                p.wait("tensor", "sac", None)
                emit_proj(pB, xlt, 0)
                p.wait("vector", "spe", None)
                emit_rms(pB, kwrep)
                for h in range(16):
                    p.op("vector", "scalar_tensor_tensor",
                         scr[:, 1024 + h * 64:1024 + (h + 1) * 64],
                         kqn[:, h * 64:(h + 1) * 64], 1.0, urep[:, h * 64:(h + 1) * 64],
                         ALU.bypass, ALU.mult, accum_out=glt[:, h:h + 1],
                         edge=(h == 15))
                p.op("vector", "tensor_scalar", glt[:, :], glt[:, :], 0.125, None, ALU.mult,
                     edge=True)
                p.op("vector", "tensor_scalar", gtmp[:, :], glt[:, :], -1.0, None, ALU.mult,
                     edge=True)
                p.op("vector", "tensor_tensor", gtmp3[:, :], gtmp[:, :], glt[:, :], ALU.max,
                     inc=("sdv", 1))
                p.wait("scalar", "sdv", None)
                p.op("scalar", "activation", gtmp[:, :], gtmp3[:, :], AF.Exp, scale=-1.0,
                     edge=True)
                p.op("scalar", "activation", gtmp[:, :], gtmp[:, :], AF.Ln, bias=1.0,
                     edge=True)
                p.op("scalar", "activation", gtmp2[:, :], glt[:, :], AF.Relu, inc=("sac", 1))
                p.wait("vector", "sac", None)
                p.op("vector", "tensor_tensor", gtmp[:, :], gtmp[:, :], gtmp2[:, :], ALU.add,
                     inc=("sdv", 1))
                p.wait("tensor", "sdv", None)
                p.op("tensor", "transpose", pD[0:16, 0:128], gtmp[:, :], id_f32[:, :],
                     inc=("spe", 1))
                p.wait("vector", "spe", None)
                p.op("vector", "tensor_copy", Wg[:, t * 128:(t + 1) * 128], pD[0:16, 0:128],
                     inc=("sdv", 1))
                p.wait("tensor", "sdv", None)
                for par in range(2):
                    for k in range(8):
                        h = 2 * k + par
                        p.op("tensor", "transpose",
                             pT[0:64, par * 1024 + k * 128: par * 1024 + (k + 1) * 128],
                             kqn[:, h * 64:(h + 1) * 64], id_bf[:, :],
                             inc=("spe", 1) if (par == 1 and k == 7) else None)
                p.wait("vector", "spe", None)
                for par in range(2):
                    dst = KT[64 * par:64 * par + 64, :].rearrange("q (c n) -> q c n", n=T)
                    src = pT[0:64, par * 1024:par * 1024 + 1024].rearrange("q (c n) -> q c n", n=128)
                    p.op("vector", "tensor_copy", dst[:, :, t * 128:(t + 1) * 128], src[:, :, :],
                         inc=("sdv", 1))
                p.wait("tensor", "sdv", None)
                emit_proj(pA, xlt, 1)
                p.wait("scalar", "spe", None)
                p.op("scalar", "activation", Vtm[:, t * 1024:t * 1024 + 512], pA[:, 0:512],
                     AF.Copy, bias=0.0)
                p.op("scalar", "activation", Vtm[:, t * 1024 + 512:(t + 1) * 1024],
                     pA[:, 512:1024], AF.Copy, bias=0.0, inc=("sac", 1))
            p.wait("vector", "sld", None)
            p.op("vector", "tensor_scalar", Wg[:, :], Wg[:, :], om8col[:, :], None, ALU.mult,
                 inc=("sdv", 1))
            iper = min(512, T)
            p.op("gpsimd", "iota", jW8[:, :], [[0, T // iper], [1, iper]], base=0,
                 channel_multiplier=0, allow_small_or_imprecise_dtypes=True,
                 inc=("sio", 1))
            p.wait("vector", "sio", None)
            p.op("vector", "tensor_tensor", jW8[:, :], jW8[:, :], Wg[:, :], ALU.mult,
                 inc=("sdv", 1))
            p.wait("sync", "sdv", None)
            p.op("sync", "dma_start", pairs_dram[0, :, :], jW8[:, :], inc=("sld", 16))
            p.op("sync", "dma_start", pairs_dram[1, :, :], Wg[:, :], inc=("sld", 16))

        with p.block(f"s{s}_p34"):
            p.wait("gpsimd", "spe", None)
            for slot, wsrc in ((0, wq_d), (1, wp_d)):
                p.op("gpsimd", "dma_start",
                     wbuf[:, slot * 8192:(slot + 1) * 8192].rearrange("p (c n) -> p c n", c=8),
                     wsrc.ap().rearrange("(c p) n -> p c n", p=128), inc=("sldg", 16))
            wready = p.val("sldg")
            pairs_ready = p.val("sld")
            for qi in range(NTILE):
                if SAFE_BLOCKS:
                    barrier()
                xb = xbuf[qi % 2]
                p.wait("sync", "sst", None)
                p.op("sync", "dma_start", xb[:, :],
                     x[s * T + qi * 128: s * T + (qi + 1) * 128, :], inc=("sld", 16))
                p.wait("scalar", "sld", None)
                p.wait("vector", "sld", None)
                emit_ln(xb, xln_bf)
                p.wait("tensor", "sdv", None)
                p.wait("tensor", "sac", None)
                if qi == 0:
                    p.wait("tensor", "sldg", wready)
                emit_transpose8(xln_bf, xlt)
                p.wait("tensor", "sdv", None)
                emit_proj(pA, xlt, 0)
                p.wait("vector", "spe", None)
                emit_rms(pA, qwrep)
                nj = qi // 4 + 1
                jlen = nj * 512
                for h in ([] if DBG_SKIP_HEADS else range(16)):
                    if SAFE_BLOCKS:
                        barrier()
                    par = h % 2
                    base = 64 * par
                    step = qi * 16 + h
                    pf = step + 3 if qi == 0 and h < 13 or True else None
                    if step == 0:
                        for s0 in range(3):
                            p.wait("sync", "spe", None)
                            p.op("sync", "dma_start", pairsH[s0 % 4][:, :],
                                 pairs_dram[:, s0 % 16, :], inc=("sld", 16))
                            pairs_ld[s0] = p.val("sld")
                    if step + 3 < 256:
                        p.wait("sync", "spe", None)
                        p.op("sync", "dma_start", pairsH[(step + 3) % 4][:, :],
                             pairs_dram[:, (step + 3) % 16, :], inc=("sld", 16))
                        pairs_ld[step + 3] = p.val("sld")
                    p.wait("tensor", "sdv", None)
                    p.wait("tensor", "sac", None)
                    p.op("tensor", "transpose", pT[0:64, 0:128], kqn[:, h * 64:(h + 1) * 64],
                         id_bf[:, :], inc=("spe", 1))
                    p.wait("vector", "spe", None)
                    p.op("vector", "tensor_copy", qs[h % 2][base:base + 64, :], pT[0:64, 0:128],
                         inc=("sdv", 1))
                    p.wait("tensor", "sdv", None)
                    p.wait("tensor", "sld", pairs_ld[step])
                    for jt in range(nj):
                        half = jt % 2
                        pbh = pB[:, half * 512:(half + 1) * 512]
                        p.wait("tensor", "sac", None)
                        p.op("tensor", "matmul", pbh, qs[h % 2][base:base + 64, :],
                             KT[base:base + 64, (h // 2) * T + jt * 512:
                                (h // 2) * T + (jt + 1) * 512],
                             start=True, stop=False)
                        p.op("tensor", "matmul", pbh,
                             lio4[:, qi * 128 - 512 * jt: qi * 128 - 512 * jt + 128],
                             pairsH[step % 4][:, jt * 512:(jt + 1) * 512],
                             start=False, stop=True, inc=("spe", 1))
                        p.wait("scalar", "spe", None)
                        p.op("scalar", "activation", sc[:, jt * 512:(jt + 1) * 512], pbh,
                             AF.Copy, bias=0.0, scale=0.125, inc=("sac", 1))
                    p.wait("vector", "sac", None)
                    p.op("vector", "tensor_tensor", sc[:, (nj - 1) * 512: jlen],
                         sc[:, (nj - 1) * 512: jlen], masks[:, (qi % 4) * 512:(qi % 4 + 1) * 512],
                         ALU.add)
                    p.op("vector", "tensor_reduce", nmax[:, :], sc[:, 0:jlen], AX.X, ALU.max,
                         negate=True, inc=("sdv", 1))
                    p.wait("scalar", "sdv", None)
                    p.op("scalar", "activation", probs[:, 0:jlen], sc[:, 0:jlen], AF.Exp,
                         bias=nmax[:, :], accum_out=rsum[:, :], inc=("sac", 1))
                    p.wait("vector", "sac", None)
                    p.op("vector", "reciprocal", rinv[:, :], rsum[:, :], inc=("sdv", 1))
                    ngrp = nj
                    for g in range(ngrp):
                        half = g % 2
                        p.wait("tensor", "sdv", None)
                        for c in range(4):
                            jc = g * 4 + c
                            p.op("tensor", "transpose",
                                 pT[:, half * 1024 + c * 128: half * 1024 + (c + 1) * 128],
                                 probs[:, jc * 128:(jc + 1) * 128], id_bf[:, :],
                                 inc=("spe", 1) if c == 3 else None)
                        p.wait("vector", "spe", None)
                        p.op("vector", "tensor_copy", ptT[:, g * 512:(g + 1) * 512],
                             pT[:, half * 1024:half * 1024 + 512], inc=("sdv", 1))
                    p.wait("tensor", "sdv", None)
                    nch = nj * 4
                    for jc in range(nch):
                        p.op("tensor", "matmul", pC[:, 0:64], ptT[:, jc * 128:(jc + 1) * 128],
                             Vtm[:, jc * 1024 + h * 64: jc * 1024 + (h + 1) * 64],
                             start=(jc == 0), stop=(jc == nch - 1),
                             inc=("spe", 1) if jc == nch - 1 else None)
                    p.wait("scalar", "spe", None)
                    p.wait("scalar", "sdv", None)
                    p.op("scalar", "activation", y_sb[:, h * 64:(h + 1) * 64], pC[:, 0:64],
                         AF.Copy, bias=0.0, scale=rinv[:, :], inc=("sac", 1))
                if DBG_SKIP_HEADS:
                    continue
                p.wait("tensor", "sac", None)
                emit_transpose8(y_sb, yT)
                p.wait("tensor", "sdv", None)
                emit_proj(pA, yT, 1)
                p.wait("scalar", "spe", None)
                p.wait("vector", "spe", None)
                p.op("scalar", "activation", probs[:, 0:1024], pA[:, :], AF.Copy, bias=0.0,
                     accum_out=ssum[:, :])
                p.op("scalar", "activation", scr[:, 0:1024], pA[:, :], AF.Square,
                     accum_out=ssq[:, :], inc=("sac", 1))
                p.wait("vector", "sac", None)
                p.op("vector", "tensor_scalar", smean[:, :], ssum[:, :], 1.0 / 1024, None,
                     ALU.mult, edge=True)
                p.op("vector", "tensor_tensor", msq[:, :], smean[:, :], smean[:, :], ALU.mult)
                p.op("vector", "tensor_scalar", ssq[:, :], ssq[:, :], 1.0 / 1024, None,
                     ALU.mult, edge=True)
                p.op("vector", "tensor_tensor", srstd[:, :], ssq[:, :], msq[:, :],
                     ALU.subtract, inc=("sdv", 1))
                p.wait("scalar", "sdv", None)
                p.op("scalar", "activation", srstd[:, :], srstd[:, :], AF.Ln, bias=epsc[:, :],
                     edge=True)
                p.op("scalar", "activation", srstd[:, :], srstd[:, :], AF.Exp, scale=-0.5,
                     inc=("sac", 1))
                p.wait("vector", "sac", None)
                p.op("vector", "tensor_scalar", scr[:, 1024:2048], pA[:, :], smean[:, :],
                     srstd[:, :], ALU.subtract, ALU.mult)
                p.op("vector", "tensor_tensor", xb[:, :], scr[:, 1024:2048],
                     xb[:, :], ALU.add, inc=("sdv", 1))
                p.wait("sync", "sdv", None)
                p.op("sync", "dma_start", y[s * T + qi * 128: s * T + (qi + 1) * 128, :],
                     xb[:, :], inc=("sst", 16))
            p.wait("sync", "sst", None)

    if dbg:
        dbg_tensors = dict(KT=KT, Vtm=Vtm, kqn=kqn, Wg=Wg, jW8=jW8, hrstd=hrstd,
                           scr=scr, probs=probs, ptT=ptT, y_sb=y_sb, yT=yT,
                           xlt=xlt, xln_bf=xln_bf, qwrep=qwrep, kwrep=kwrep,
                           urep=urep, om8col=om8col, rinv=rinv, glt=glt)
        with p.block("dbg"):
            for nm in dbg:
                t = dbg_tensors[nm]
                shp = list(t.shape)
                od = nc.dram_tensor(f"dbg_{nm}", shp, t.dtype, kind="ExternalOutput")
                p.op("sync", "dma_start", od[:, :], t[:, :], inc=("sst", 16))
            p.wait("sync", "sst", None)

    st.close()
    return nc


def inputs_b(x_shard, w):
    m = {}
    m["x"] = np.ascontiguousarray(x_shard, dtype=np.float32)
    m["rw"] = w["L0_ffn_router_W"]
    m["rb"] = np.ascontiguousarray(w["L0_ffn_router_b"]).reshape(1, 8)
    m["w1"] = w["L0_ffn_W1"]
    m["b1"] = w["L0_ffn_b1"]
    m["w2"] = w["L0_ffn_W2"]
    m["b2"] = w["L0_ffn_b2"]
    ca = make_consts_a()
    m["cid_f32"] = ca["cid_f32"]
    m["ceps"] = ca["ceps"]
    m["cones1_f32"] = ca["cones1_f32"]
    return m


def build_b(NTILE=32, NEXP=8):
    nc = bass.Bass("TRN2", target_bir_lowering=False)
    nc.detect_race_conditions = False
    st = contextlib.ExitStack()
    p = Prog(nc, st)

    NT = NTILE * 128
    assert NT % 512 == 0
    NTQ = NT // 512

    x = nc.dram_tensor("x", [NT, 1024], F32, kind="ExternalInput")
    rw_d = nc.dram_tensor("rw", [1024, 8], F32, kind="ExternalInput")
    rb_d = nc.dram_tensor("rb", [1, 8], F32, kind="ExternalInput")
    w1_d = nc.dram_tensor("w1", [8, 1024, 4096], F32, kind="ExternalInput")
    b1_d = nc.dram_tensor("b1", [8, 4096], F32, kind="ExternalInput")
    w2_d = nc.dram_tensor("w2", [8, 4096, 1024], F32, kind="ExternalInput")
    b2_d = nc.dram_tensor("b2", [8, 1024], F32, kind="ExternalInput")
    id_f32_d = nc.dram_tensor("cid_f32", [128, 128], F32, kind="ExternalInput")
    eps_d = nc.dram_tensor("ceps", [128, 1], F32, kind="ExternalInput")
    ones1f_d = nc.dram_tensor("cones1_f32", [1, 128], F32, kind="ExternalInput")
    y = nc.dram_tensor("y", [NT, 1024], F32, kind="ExternalOutput")
    y2 = nc.dram_tensor("y2", [NT, 1024], F32, kind="ExternalOutput")
    fout = nc.dram_tensor("fout", [NT, 1024], F32)

    sb = lambda name, shape, dt: st.enter_context(nc.sbuf_tensor(name, shape, dt))
    id_f32 = sb("id_f32", [128, 128], F32)
    epsc = sb("epsc", [128, 1], F32)
    ones1f = sb("ones1f", [1, 128], F32)
    rwsb = sb("rwsb", [128, 8 * 8], F32)
    rbrow = sb("rbrow", [1, 8], F32)
    brep = sb("brep", [128, 8], F32)
    b2nat = sb("b2nat", [8, 1024], F32)
    b1sb = sb("b1sb", [128, 32 * 8], F32)
    xlnT = sb("xlnT", [128, 8 * NT], BF16)
    gates_tm = sb("gates_tm", [128, NTILE * 8], F32)
    w1c = [sb(f"w1c{i}", [128, 8 * 128], BF16) for i in range(2)]
    xbuf = [sb(f"xbuf{i}", [128, 1024], F32) for i in range(2)]
    xln_f32 = sb("xln_f32", [128, 1024], F32)
    xlTf = sb("xlTf", [128, 1024], F32)
    scr = sb("scr", [128, 1024], F32)
    oscr = [sb(f"oscr{i}", [128, 1024], F32) for i in range(2)]
    fbuf = oscr
    ybuf = [sb(f"ybuf{i}", [128, 1024], F32) for i in range(2)]
    y2buf = [xln_f32, xlTf]
    lg = sb("lg", [128, 8], F32)
    lge = sb("lge", [128, 8], F32)
    gmask = sb("gmask", [128, 8], F32)
    ssum = sb("ssum", [128, 1], F32)
    ssq = sb("ssq", [128, 1], F32)
    smean = sb("smean", [128, 1], F32)
    srstd = sb("srstd", [128, 1], F32)
    msq = sb("msq", [128, 1], F32)
    nmax = sb("nmax", [128, 1], F32)
    rsum = sb("rsum", [128, 1], F32)
    rinv = sb("rinv", [128, 1], F32)

    psd = lambda name, shape, dt: st.enter_context(nc.psum_tensor(name, shape, dt))
    pA = psd("pA", [128, 1024], F32)
    pB = psd("pB", [128, 1024], F32)
    pH = psd("pH", [128, 1024], F32)
    pS = psd("pS", [128, 512], F32)

    for s in ("sld", "sldg", "sio", "sdv", "sac", "spe", "sst", "sfo"):
        p.sem(s)
    p.clear_sems_block()

    def emit_stats(src_ap, dump_bf):
        p.op("scalar", "activation", dump_bf, src_ap, AF.Copy, bias=0.0,
             accum_out=ssum[:, :])
        p.op("scalar", "activation", scr[:, :], src_ap, AF.Square,
             accum_out=ssq[:, :], inc=("sac", 1))
        p.wait("vector", "sac", None)
        p.op("vector", "tensor_scalar", smean[:, :], ssum[:, :], 1.0 / 1024, None, ALU.mult,
             edge=True)
        p.op("vector", "tensor_tensor", msq[:, :], smean[:, :], smean[:, :], ALU.mult)
        p.op("vector", "tensor_scalar", ssq[:, :], ssq[:, :], 1.0 / 1024, None, ALU.mult,
             edge=True)
        p.op("vector", "tensor_tensor", srstd[:, :], ssq[:, :], msq[:, :], ALU.subtract,
             inc=("sdv", 1))
        p.wait("scalar", "sdv", None)
        p.op("scalar", "activation", srstd[:, :], srstd[:, :], AF.Ln, bias=epsc[:, :],
             edge=True)
        p.op("scalar", "activation", srstd[:, :], srstd[:, :], AF.Exp, scale=-0.5,
             inc=("sac", 1))
        p.wait("vector", "sac", None)

    st2 = contextlib.ExitStack()
    sb2 = lambda name, shape, dt: st2.enter_context(nc.sbuf_tensor(name, shape, dt))
    b1nat = sb2("b1nat", [8, 4096], F32)
    gatesT = sb2("gatesT", [8, NT], F32)

    with p.block("bsetup"):
        for dst, srcd in ((id_f32, id_f32_d), (epsc, eps_d), (ones1f, ones1f_d),
                          (rbrow, rb_d), (b1nat, b1_d), (b2nat, b2_d)):
            p.op("sync", "dma_start", dst[:, :], srcd[:, :], inc=("sld", 16))
        p.op("sync", "dma_start", rwsb[:, :].rearrange("p (c e) -> p c e", e=8),
             rw_d.ap().rearrange("(c p) e -> p c e", p=128), inc=("sld", 16))
        p.wait("tensor", "sld", None)
        p.op("tensor", "matmul", pS[:, 0:8], ones1f[:, :], rbrow[:, :], start=True,
             stop=True, inc=("spe", 1))
        p.wait("vector", "spe", None)
        p.op("vector", "tensor_copy", brep[:, :], pS[:, 0:8], inc=("sdv", 1))
        for g in range(4):
            p.wait("tensor", "sdv", None)
            for k in range(8):
                fc = g * 8 + k
                p.op("tensor", "transpose", pH[:, k * 128:k * 128 + 8],
                     b1nat[:, fc * 128:(fc + 1) * 128], id_f32[0:8, 0:8],
                     inc=("spe", 1) if k == 7 else None)
            p.wait("vector", "spe", None)
            p.op("vector", "tensor_copy",
                 b1sb[:, g * 64:(g + 1) * 64].rearrange("p (c e) -> p c e", e=8),
                 pH[:, 0:1024].rearrange("p (c k) -> p c k", k=128)[:, :, 0:8],
                 inc=("sdv", 1))
        p.wait("tensor", "sdv", None)

    with p.block("b0"):
        for t in range(NTILE):
            xb = xbuf[t % 2]
            p.wait("sync", "sdv", None)
            p.op("sync", "dma_start", xb[:, :], x[t * 128:(t + 1) * 128, :], inc=("sld", 16))
            p.wait("scalar", "sld", None)
            p.wait("vector", "sld", None)
            emit_stats(xb[:, :], xln_f32[:, :])
            p.op("vector", "tensor_scalar", xln_f32[:, :], xb[:, :], smean[:, :],
                 srstd[:, :], ALU.subtract, ALU.mult, inc=("sdv", 1))
            p.wait("tensor", "sdv", None)
            for half in range(2):
                for c4 in range(4):
                    c = half * 4 + c4
                    p.op("tensor", "transpose", pH[:, half * 512 + c4 * 128: half * 512 + (c4 + 1) * 128],
                         xln_f32[:, c * 128:(c + 1) * 128], id_f32[:, :],
                         inc=("spe", 1) if c4 == 3 else None)
                p.wait("vector", "spe", None)
                p.op("vector", "tensor_copy", xlTf[:, half * 512:(half + 1) * 512],
                     pH[:, half * 512:(half + 1) * 512])
                p.op("vector", "tensor_copy",
                     xlnT[:, :].rearrange("p (c n) -> p c n", n=NT)[:, half * 4:(half + 1) * 4,
                                                                   t * 128:(t + 1) * 128],
                     pH[:, half * 512:(half + 1) * 512].rearrange("p (c n) -> p c n", n=128),
                     inc=("sdv", 1))
                p.wait("tensor", "sdv", None)
            for c in range(8):
                p.op("tensor", "matmul", pS[:, 0:8], xlTf[:, c * 128:(c + 1) * 128],
                     rwsb[:, c * 8:(c + 1) * 8], start=(c == 0), stop=(c == 7),
                     inc=("spe", 1) if c == 7 else None)
            p.wait("vector", "spe", None)
            p.op("vector", "tensor_tensor", lg[:, :], pS[:, 0:8], brep[:, :], ALU.add,
                 edge=True)
            p.op("vector", "tensor_reduce", nmax[:, :], lg[:, :], AX.X, ALU.max,
                 negate=True, inc=("sdv", 1))
            p.wait("scalar", "sdv", None)
            p.op("scalar", "activation", lge[:, :], lg[:, :], AF.Exp, bias=nmax[:, :],
                 accum_out=rsum[:, :], inc=("sac", 1))
            p.wait("vector", "sac", None)
            p.op("vector", "reciprocal", rinv[:, :], rsum[:, :], edge=True)
            p.op("vector", "tensor_scalar", gmask[:, :], lge[:, :], 1.0, None, ALU.is_ge,
                 edge=True)
            p.op("vector", "tensor_scalar", lge[:, :], lge[:, :], rinv[:, :], None, ALU.mult,
                 edge=True)
            p.op("vector", "tensor_tensor", gates_tm[:, t * 8:(t + 1) * 8], gmask[:, :],
                 lge[:, :], ALU.mult, inc=("sdv", 1))
            p.wait("tensor", "sdv", None)
            p.op("tensor", "transpose", pS[0:8, 128:256], gates_tm[:, t * 8:(t + 1) * 8],
                 id_f32[:, :], inc=("spe", 1))
            p.wait("vector", "spe", None)
            p.op("vector", "tensor_copy", gatesT[:, t * 128:(t + 1) * 128], pS[0:8, 128:256],
                 inc=("sdv", 1))
            p.wait("tensor", "sdv", None)
        for t in range(NTILE):
            pp = pA if t % 2 == 0 else pB
            osb = oscr[t % 2]
            p.wait("tensor", "sfo", None)
            for nh in range(2):
                p.op("tensor", "matmul", pp[:, nh * 512:(nh + 1) * 512],
                     gatesT[:, t * 128:(t + 1) * 128], b2nat[:, nh * 512:(nh + 1) * 512],
                     start=True, stop=True, inc=("spe", 1) if nh == 1 else None)
            p.wait("vector", "spe", None)
            p.wait("vector", "sfo", None)
            p.op("vector", "tensor_copy", osb[:, :], pp[:, :], inc=("sdv", 1))
            p.wait("gpsimd", "sdv", None)
            p.op("gpsimd", "dma_start", fout[t * 128:(t + 1) * 128, :], osb[:, :],
                 inc=("sfo", 16))

    st2.close()
    w2sb = sb("w2sb", [128, 32 * 1024], BF16)
    h_sb = sb("h_sb", [128, 32 * 512], BF16)

    with p.block("bmain"):
        for e in range(NEXP):
            p.wait("gpsimd", "spe", None)
            p.op("gpsimd", "dma_start",
                 w2sb[:, :].rearrange("p (c n) -> p c n", n=1024),
                 w2_d[e].rearrange("(c p) n -> p c n", p=128), inc=("sldg", 16))
            w2ready = p.val("sldg")
            for tq in range(NTQ):
                for fc in range(32):
                    wb = w1c[fc % 2]
                    p.wait("gpsimd", "spe", None)
                    p.op("gpsimd", "dma_start",
                         wb[:, :].rearrange("p (c n) -> p c n", n=128),
                         w1_d[e][:, fc * 128:(fc + 1) * 128].rearrange("(c p) n -> p c n", p=128),
                         inc=("sldg", 16))
                    wv = p.val("sldg")
                    p.wait("tensor", "sldg", wv)
                    p.wait("tensor", "sac", None)
                    ph = pH[:, (fc % 2) * 512:(fc % 2 + 1) * 512]
                    for dc in range(8):
                        p.op("tensor", "matmul", ph,
                             wb[:, dc * 128:(dc + 1) * 128],
                             xlnT[:, :].rearrange("p (c n) -> p c n", n=NT)
                                 [:, dc, tq * 512:(tq + 1) * 512],
                             start=(dc == 0), stop=(dc == 7),
                             inc=("spe", 1) if dc == 7 else None)
                    p.wait("scalar", "spe", None)
                    p.op("scalar", "activation",
                         h_sb[:, fc * 512:(fc + 1) * 512], ph, AF.Relu,
                         bias=b1sb[:, fc * 8 + e: fc * 8 + e + 1], inc=("sac", 1))
                if tq == 0:
                    p.wait("tensor", "sldg", w2ready)
                for tt in range(4):
                    gt = tq * 4 + tt
                    pp = pA if gt % 2 == 0 else pB
                    osb = oscr[gt % 2]
                    p.wait("tensor", "sdv", None)
                    p.wait("tensor", "sac", None)
                    for fc in range(32):
                        for nh in range(2):
                            p.op("tensor", "matmul",
                                 pp[:, nh * 512:(nh + 1) * 512],
                                 h_sb[:, fc * 512 + tt * 128: fc * 512 + (tt + 1) * 128],
                                 w2sb[:, fc * 1024 + nh * 512: fc * 1024 + (nh + 1) * 512],
                                 start=(fc == 0), stop=(fc == 31),
                                 inc=("spe", 1) if (fc == 31 and nh == 1) else None)
                    p.wait("vector", "spe", None)
                    p.wait("vector", "sfo", None)
                    p.op("vector", "tensor_scalar", osb[:, :], pp[:, :],
                         gates_tm[:, gt * 8 + e: gt * 8 + e + 1], None, ALU.mult,
                         inc=("sdv", 1))
                    p.wait("gpsimd", "sdv", None)
                    p.op("gpsimd", "dma_start", fout[gt * 128:(gt + 1) * 128, :],
                         osb[:, :], accum_op=ALU.add, inc=("sfo", 16))

    with p.block("bexit"):
        p.wait("sync", "sfo", None)
        for t in range(NTILE):
            xb = xbuf[t % 2]
            fb = fbuf[t % 2]
            yb = ybuf[t % 2]
            y2b = y2buf[t % 2]
            p.wait("sync", "sst", None)
            p.op("sync", "dma_start", xb[:, :], x[t * 128:(t + 1) * 128, :], inc=("sld", 16))
            p.op("sync", "dma_start", fb[:, :], fout[t * 128:(t + 1) * 128, :],
                 inc=("sld", 16))
            p.wait("scalar", "sld", None)
            p.wait("vector", "sld", None)
            emit_stats(fb[:, :], scr[:, :])
            p.op("vector", "tensor_scalar", scr[:, :], fb[:, :], smean[:, :], srstd[:, :],
                 ALU.subtract, ALU.mult, edge=True)
            p.op("vector", "tensor_tensor", yb[:, :], scr[:, :], xb[:, :], ALU.add,
                 inc=("sdv", 1))
            p.wait("scalar", "sdv", None)
            emit_stats(yb[:, :], scr[:, :])
            p.op("vector", "tensor_scalar", y2b[:, :], yb[:, :], smean[:, :], srstd[:, :],
                 ALU.subtract, ALU.mult, inc=("sdv", 1))
            p.wait("sync", "sdv", None)
            p.op("sync", "dma_start", y[t * 128:(t + 1) * 128, :], yb[:, :], inc=("sst", 16))
            p.op("sync", "dma_start", y2[t * 128:(t + 1) * 128, :], y2b[:, :], inc=("sst", 16))
        p.wait("sync", "sst", None)

    st.close()
    return nc


def inputs_c(ln2_slice, w):
    m = {}
    m["x"] = np.ascontiguousarray(ln2_slice, dtype=np.float32).reshape(-1, 128)
    m["wq"] = w["L1_attn_Wq"]
    m["wk"] = w["L1_attn_Wk"]
    m["wv"] = w["L1_attn_Wv"]
    ca = make_consts_a()
    m["cid_f32"] = ca["cid_f32"]
    m["cid_bf"] = ca["cid_bf"]
    m["cones_bf"] = np.ones((128, 128), dtype=ml_dtypes.bfloat16)
    return m


def build_c(NC=16, NTILE=16):
    nc = bass.Bass("TRN2", target_bir_lowering=False)
    nc.detect_race_conditions = False
    st = contextlib.ExitStack()
    p = Prog(nc, st)
    T = NTILE * 128

    x = nc.dram_tensor("x", [NC * T, 128], F32, kind="ExternalInput")
    wq_d = nc.dram_tensor("wq", [64, 64], F32, kind="ExternalInput")
    wk_d = nc.dram_tensor("wk", [64, 64], F32, kind="ExternalInput")
    wv_d = nc.dram_tensor("wv", [64, 64], F32, kind="ExternalInput")
    id_f32_d = nc.dram_tensor("cid_f32", [128, 128], F32, kind="ExternalInput")
    id_bf_d = nc.dram_tensor("cid_bf", [128, 128], BF16, kind="ExternalInput")
    ones_bf_d = nc.dram_tensor("cones_bf", [128, 128], BF16, kind="ExternalInput")
    o = nc.dram_tensor("o", [NC * T, 128], F32, kind="ExternalOutput")

    sb = lambda name, shape, dt: st.enter_context(nc.sbuf_tensor(name, shape, dt))
    id_f32 = sb("id_f32", [128, 128], F32)
    id_bf = sb("id_bf", [128, 128], BF16)
    ones_bf = sb("ones_bf", [128, 128], BF16)
    wt = sb("wt", [64, 192], F32)
    wqT = sb("wqT", [128, 64], BF16)
    wkT = sb("wkT", [128, 64], BF16)
    wvT = sb("wvT", [128, 64], BF16)
    xbuf = [sb(f"xbuf{i}", [128, 128], F32) for i in range(2)]
    xtb = sb("xtb", [128, 128], BF16)
    xrT = sb("xrT", [128, 128], BF16)
    qsb = sb("qsb", [128, 128], BF16)
    vsb = sb("vsb", [128, 16 * 128], BF16)
    NDS = 2 * NTILE * NC
    Bsb = sb("Bsb", [128, NC * 128], BF16)
    DS = sb("DS", [128, NDS], F32)
    DSS = sb("DSS", [128, NDS], F32)
    mx32 = sb("mx32", [128, 2 * NTILE], F32)
    sm32 = sb("sm32", [128, 2 * NTILE], F32)
    dcol = sb("dcol", [128, 64], F32)
    osb = [sb(f"osb{i}", [128, 128], F32) for i in range(2)]

    psd = lambda name, shape, dt: st.enter_context(nc.psum_tensor(name, shape, dt))
    pQ = [psd("pQ0", [128, 512], F32), psd("pQ1", [128, 512], F32)]
    pK = [psd("pK0", [128, 512], F32), psd("pK1", [128, 512], F32)]
    pV = [psd("pV0", [128, 512], F32), psd("pV1", [128, 512], F32)]
    pBc = psd("pBc", [128, 512], F32)
    pT = psd("pT", [128, 512], BF16)
    pW = pBc

    for s in ("sld", "sldg", "sio", "sdv", "sac", "spe", "sst"):
        p.sem(s)
    p.clear_sems_block()

    with p.block("csetup"):
        for i, wsrc in enumerate((wq_d, wk_d, wv_d)):
            p.op("sync", "dma_start", wt[:, i * 64:(i + 1) * 64], wsrc[:, :],
                 inc=("sld", 16))
        for dst, srcd in ((id_f32, id_f32_d), (id_bf, id_bf_d), (ones_bf, ones_bf_d)):
            p.op("sync", "dma_start", dst[:, :], srcd[:, :], inc=("sld", 16))
        p.wait("tensor", "sld", None)
        for i, wdst in enumerate((wqT, wkT, wvT)):
            p.op("tensor", "transpose", pW[0:64, 0:64], wt[:, i * 64:(i + 1) * 64],
                 id_f32[0:64, 0:64], inc=("spe", 1))
            p.wait("vector", "spe", None)
            p.op("vector", "tensor_copy", wdst[0:64, :], pW[0:64, 0:64])
            p.op("vector", "tensor_copy", wdst[64:128, :], pW[0:64, 0:64], inc=("sdv", 1))
            p.wait("tensor", "sdv", None)

    with p.block("cpassA"):
        for c in range(NC):
            for tt in range(NTILE):
                xb = xbuf[tt % 2]
                p.wait("sync", "sdv", None)
                p.op("sync", "dma_start", xb[:, :],
                     x[(c * T + tt * 128):(c * T + (tt + 1) * 128), :], inc=("sld", 16))
                p.wait("vector", "sld", None)
                p.op("vector", "tensor_copy", xtb[:, :], xb[:, :], inc=("sdv", 1))
                p.wait("tensor", "sdv", None)
                p.wait("tensor", "sac", None)
                for h in range(2):
                    p.op("tensor", "transpose", pT[0:64, h * 128:(h + 1) * 128],
                         xtb[:, h * 64:(h + 1) * 64], id_bf[:, :],
                         inc=("spe", 1) if h == 1 else None)
                p.wait("vector", "spe", None)
                p.op("vector", "tensor_copy", xrT[0:64, :],
                     pT[0:64, 0:128])
                p.op("vector", "tensor_copy", xrT[64:128, :],
                     pT[0:64, 128:256], inc=("sdv", 1))
                p.wait("tensor", "sdv", None)
                if DBG_C_CUT < 1:
                    continue
                for h in range(2):
                    b = 64 * h
                    p.op("tensor", "matmul", pQ[h][:, 0:64],
                         xrT[b:b + 64, :], wqT[b:b + 64, :], start=True, stop=True)
                    p.op("tensor", "matmul", pK[h][:, 0:64],
                         xrT[b:b + 64, :], wkT[b:b + 64, :], start=True, stop=True)
                    p.op("tensor", "matmul", pV[h][:, 0:64],
                         xrT[b:b + 64, :], wvT[b:b + 64, :], start=True, stop=True,
                         inc=("spe", 1) if h == 1 else None)
                p.wait("scalar", "spe", None)
                for h in range(2):
                    p.op("scalar", "activation", qsb[:, h * 64:(h + 1) * 64],
                         pQ[h][:, 0:64], AF.Copy, bias=0.0)
                    p.op("scalar", "activation",
                         vsb[:, tt * 128 + h * 64: tt * 128 + (h + 1) * 64],
                         pV[h][:, 0:64], AF.Copy, bias=0.0,
                         inc=("sac", 1) if h == 1 else None)
                p.wait("vector", "sac", None)
                if DBG_C_CUT < 2:
                    continue
                for h in range(2):
                    p.op("vector", "scalar_tensor_tensor", dcol[:, :],
                         pK[h][:, 0:64], 1.0, qsb[:, h * 64:(h + 1) * 64],
                         ALU.bypass, ALU.mult,
                         accum_out=DS[:, h * NTILE * NC + tt * NC + c:
                                      h * NTILE * NC + tt * NC + c + 1],
                         inc=("sdv", 1) if h == 1 else None)
            if DBG_C_CUT < 3:
                continue
            p.wait("tensor", "sac", None)
            for tt in range(NTILE):
                p.op("tensor", "matmul", pBc[:, 0:128], ones_bf[:, :],
                     vsb[:, tt * 128:(tt + 1) * 128],
                     start=(tt == 0), stop=(tt == NTILE - 1),
                     inc=("spe", 1) if tt == NTILE - 1 else None)
            p.wait("scalar", "spe", None)
            p.op("scalar", "activation", Bsb[:, c * 128:(c + 1) * 128], pBc[:, 0:128],
                 AF.Copy, bias=0.0, inc=("sac", 1))
            p.wait("tensor", "sac", None)

    if DBG_C_PHASES < 2:
        st.close()
        return nc
    with p.block("csm"):
        p.op("vector", "tensor_scalar", DSS[:, :], DS[:, :], 1.0 / 32.0, None, ALU.mult,
             edge=True)
        d3 = DSS[:, :].rearrange("p (g c) -> p g c", c=NC)
        p.op("vector", "tensor_reduce", mx32[:, :], d3, AX.X, ALU.max, negate=True,
             edge=True)
        for g in range(2 * NTILE):
            p.op("vector", "tensor_scalar", DSS[:, g * NC:(g + 1) * NC],
                 DSS[:, g * NC:(g + 1) * NC], mx32[:, g:g + 1], None, ALU.add,
                 inc=("sdv", 1) if g == 2 * NTILE - 1 else None)
        p.wait("scalar", "sdv", None)
        p.op("scalar", "activation", DSS[:, :], DSS[:, :], AF.Exp, inc=("sac", 1))
        p.wait("vector", "sac", None)
        p.op("vector", "tensor_reduce", sm32[:, :],
             DSS[:, :].rearrange("p (g c) -> p g c", c=NC), AX.X, ALU.add, edge=True)
        p.op("vector", "reciprocal", mx32[:, :], sm32[:, :], edge=True)
        for g in range(2 * NTILE):
            p.op("vector", "tensor_scalar", DSS[:, g * NC:(g + 1) * NC],
                 DSS[:, g * NC:(g + 1) * NC], mx32[:, g:g + 1], None, ALU.mult,
                 inc=("sdv", 1) if g == 2 * NTILE - 1 else None)

    if DBG_C_PHASES < 3:
        st.close()
        return nc
    with p.block("cpassB"):
        for c in range(NC):
            for tt in range(NTILE):
                ob = osb[tt % 2]
                p.wait("vector", "sst", None)
                for h in range(2):
                    p.op("vector", "tensor_scalar", ob[:, h * 64:(h + 1) * 64],
                         Bsb[:, c * 128 + h * 64: c * 128 + (h + 1) * 64],
                         DSS[:, h * NTILE * NC + tt * NC + c:
                             h * NTILE * NC + tt * NC + c + 1],
                         None, ALU.mult, inc=("sdv", 1) if h == 1 else None)
                p.wait("sync", "sdv", None)
                p.op("sync", "dma_start",
                     o[(c * T + tt * 128):(c * T + (tt + 1) * 128), :], ob[:, :],
                     inc=("sst", 16))
        p.wait("sync", "sst", None)

    st.close()
    return nc


def inputs_d(x2_shard, attnc_shard, w):
    m = {}
    m["x2"] = np.ascontiguousarray(x2_shard, dtype=np.float32)
    m["ac"] = np.ascontiguousarray(attnc_shard, dtype=np.float32)
    m["proj"] = w["L1_ffn_proj"]
    m["c1"] = w["L1_ffn_C1"]
    m["c2"] = w["L1_ffn_C2"]
    m["win1"] = w["L1_ffn_Win1"]
    m["wout1"] = w["L1_ffn_Wout1"]
    m["win2"] = w["L1_ffn_Win2"]
    m["wout2"] = w["L1_ffn_Wout2"]
    ca = make_consts_a()
    m["cid_f32"] = ca["cid_f32"]
    m["cid_bf"] = ca["cid_bf"]
    m["ceps"] = ca["ceps"]
    return m


def build_d(NTILE=32, NBIS=12, NNEWT=3):
    nc = bass.Bass("TRN2", target_bir_lowering=False)
    nc.detect_race_conditions = False
    st = contextlib.ExitStack()
    p = Prog(nc, st)
    NT = NTILE * 128
    assert NT % 512 == 0
    NG = NT // 512

    x2_d = nc.dram_tensor("x2", [NT, 1024], F32, kind="ExternalInput")
    ac_d = nc.dram_tensor("ac", [NT, 1024], F32, kind="ExternalInput")
    proj_d = nc.dram_tensor("proj", [1024, 16], F32, kind="ExternalInput")
    c1_d = nc.dram_tensor("c1", [16, 512], F32, kind="ExternalInput")
    c2_d = nc.dram_tensor("c2", [16, 512], F32, kind="ExternalInput")
    win1_d = nc.dram_tensor("win1", [1025, 512], F32, kind="ExternalInput")
    wout1_d = nc.dram_tensor("wout1", [4096, 512], F32, kind="ExternalInput")
    win2_d = nc.dram_tensor("win2", [4097, 512], F32, kind="ExternalInput")
    wout2_d = nc.dram_tensor("wout2", [1024, 512], F32, kind="ExternalInput")
    id_f32_d = nc.dram_tensor("cid_f32", [128, 128], F32, kind="ExternalInput")
    id_bf_d = nc.dram_tensor("cid_bf", [128, 128], BF16, kind="ExternalInput")
    eps_d = nc.dram_tensor("ceps", [128, 1], F32, kind="ExternalInput")
    o = nc.dram_tensor("o", [NT, 1024], F32, kind="ExternalOutput")
    x3_dram = nc.dram_tensor("x3_dram", [NT, 1024], F32)

    sb = lambda name, shape, dt: st.enter_context(nc.sbuf_tensor(name, shape, dt))
    id_f32 = sb("id_f32", [128, 128], F32)
    id_bf = sb("id_bf", [128, 128], BF16)
    epsc = sb("epsc", [128, 1], F32)
    xlnT = sb("xlnT", [128, 8 * NT], BF16)
    win1sb = sb("win1sb", [128, 8 * 512], BF16)
    win2sb = sb("win2sb", [128, 32 * 512], BF16)
    woutT = sb("woutT", [128, 4 * 4096], BF16)
    wout2T = sb("wout2T", [128, 4 * 1024], BF16)
    projsb = sb("projsb", [128, 8 * 16], BF16)
    c1bf = sb("c1bf", [16, 512], BF16)
    c2bf = sb("c2bf", [16, 512], BF16)
    bz1 = sb("bz1", [128, 4], F32)
    bz2 = sb("bz2", [128, 4], F32)
    wrow = sb("wrow", [1, 512], F32)
    wtmp = sb("wtmp", [128, 512], BF16)
    aT = sb("aT", [16, NT], BF16)
    g1T = sb("g1T", [128, 4 * 512], BF16)
    g2T = sb("g2T", [128, 4 * 512], BF16)
    zg1 = sb("zg1", [128, 4 * 512], BF16)
    zg2 = zg1
    hb = sb("hb", [128, 512], BF16)
    xbuf = [sb(f"xbuf{i}", [128, 1024], F32) for i in range(2)]
    abuf = [sb(f"abuf{i}", [128, 1024], F32) for i in range(2)]
    x3buf = [sb(f"x3buf{i}", [128, 1024], F32) for i in range(2)]
    obuf = xbuf
    xln_bf = sb("xln_bf", [128, 1024], BF16)
    scr = sb("scr", [128, 1024], F32)
    ssum = sb("ssum", [128, 1], F32)
    ssq = sb("ssq", [128, 1], F32)
    smean = sb("smean", [128, 1], F32)
    srstd = sb("srstd", [128, 1], F32)
    msq = sb("msq", [128, 1], F32)
    zt = sb("zt", [128, 16], F32)
    rl = sb("rl", [128, 16], F32)
    dmp16 = sb("dmp16", [128, 16], F32)
    losA = sb("losA", [128, 1], F32)
    losB = sb("losB", [128, 1], F32)
    hisA = sb("hisA", [128, 1], F32)
    hisB = sb("hisB", [128, 1], F32)
    mid = sb("mid", [128, 1], F32)
    gac = sb("gac", [128, 1], F32)
    s1c = sb("s1c", [128, 1], F32)
    mk_ = sb("mk_", [128, 1], F32)
    dd = sb("dd", [128, 1], F32)
    rec = sb("rec", [128, 1], F32)
    nmax = sb("nmax", [128, 1], F32)
    ab = sb("ab", [128, 16], BF16)

    psd = lambda name, shape, dt: st.enter_context(nc.psum_tensor(name, shape, dt))
    pA = psd("pA", [128, 1024], F32)
    pB = psd("pB", [128, 1024], F32)
    pH2 = psd("pH2", [128, 1024], F32)
    pZ = psd("pZ", [128, 512], F32)
    pT = psd("pT", [128, 1024], BF16)

    for s in ("sld", "sldg", "sio", "sdv", "sac", "spe", "sst"):
        p.sem(s)
    p.clear_sems_block()

    def emit_stats_d(src_ap, dump):
        p.op("scalar", "activation", dump, src_ap, AF.Copy, bias=0.0,
             accum_out=ssum[:, :])
        p.op("scalar", "activation", scr[:, :], src_ap, AF.Square,
             accum_out=ssq[:, :], inc=("sac", 1))
        p.wait("vector", "sac", None)
        p.op("vector", "tensor_scalar", smean[:, :], ssum[:, :], 1.0 / 1024, None, ALU.mult,
             edge=True)
        p.op("vector", "tensor_tensor", msq[:, :], smean[:, :], smean[:, :], ALU.mult)
        p.op("vector", "tensor_scalar", ssq[:, :], ssq[:, :], 1.0 / 1024, None, ALU.mult,
             edge=True)
        p.op("vector", "tensor_tensor", srstd[:, :], ssq[:, :], msq[:, :], ALU.subtract,
             inc=("sdv", 1))
        p.wait("scalar", "sdv", None)
        p.op("scalar", "activation", srstd[:, :], srstd[:, :], AF.Ln, bias=epsc[:, :],
             edge=True)
        p.op("scalar", "activation", srstd[:, :], srstd[:, :], AF.Exp, scale=-0.5,
             inc=("sac", 1))
        p.wait("vector", "sac", None)

    with p.block("dsetup"):
        for dst, srcd in ((id_f32, id_f32_d), (id_bf, id_bf_d), (epsc, eps_d)):
            p.op("sync", "dma_start", dst[:, :], srcd[:, :], inc=("sld", 16))
        p.op("gpsimd", "dma_start", projsb[:, :].rearrange("p (c e) -> p c e", e=16),
             proj_d.ap().rearrange("(c p) e -> p c e", p=128), inc=("sldg", 16))
        p.op("gpsimd", "dma_start", c1bf[:, :], c1_d[:, :], inc=("sldg", 16))
        p.op("gpsimd", "dma_start", c2bf[:, :], c2_d[:, :], inc=("sldg", 16))
        p.op("gpsimd", "dma_start", win1sb[:, :].rearrange("p (c r) -> p c r", r=512),
             win1_d[0:1024, :].rearrange("(c p) r -> p c r", p=128), inc=("sldg", 16))
        p.op("gpsimd", "dma_start", win2sb[:, :].rearrange("p (c r) -> p c r", r=512),
             win2_d[0:4096, :].rearrange("(c p) r -> p c r", p=128), inc=("sldg", 16))
        for wsrc, bdst in ((win1_d, bz1), (win2_d, bz2)):
            row = 1024 if wsrc is win1_d else 4096
            p.op("sync", "dma_start", wrow[:, :], wsrc[row:row + 1, :], inc=("sld", 16))
            p.wait("tensor", "sld", None)
            p.wait("tensor", "sdv", None)
            for k in range(4):
                p.op("tensor", "transpose", pZ[:, k:k + 1],
                     wrow[:, k * 128:(k + 1) * 128], id_f32[0:1, 0:1],
                     inc=("spe", 1) if k == 3 else None)
            p.wait("vector", "spe", None)
            p.op("vector", "tensor_copy", bdst[:, :], pZ[:, 0:4], inc=("sdv", 1))
            p.wait("sync", "sdv", None)
        for oc in range(32):
            p.wait("gpsimd", "sdv", None)
            p.op("gpsimd", "dma_start", wtmp[:, :],
                 wout1_d[oc * 128:(oc + 1) * 128, :], inc=("sldg", 16))
            p.wait("tensor", "sldg", None)
            p.wait("tensor", "sdv", None)
            for rc in range(4):
                p.op("tensor", "transpose", pT[:, rc * 128:(rc + 1) * 128],
                     wtmp[:, rc * 128:(rc + 1) * 128], id_bf[:, :],
                     inc=("spe", 1) if rc == 3 else None)
            p.wait("vector", "spe", None)
            p.op("vector", "tensor_copy",
                 woutT[:, :].rearrange("p (c n) -> p c n", n=4096)[:, :, oc * 128:(oc + 1) * 128],
                 pT[:, 0:512].rearrange("p (c n) -> p c n", n=128), inc=("sdv", 1))
        for oc in range(8):
            p.wait("gpsimd", "sdv", None)
            p.op("gpsimd", "dma_start", wtmp[:, :],
                 wout2_d[oc * 128:(oc + 1) * 128, :], inc=("sldg", 16))
            p.wait("tensor", "sldg", None)
            p.wait("tensor", "sdv", None)
            for rc in range(4):
                p.op("tensor", "transpose", pT[:, rc * 128:(rc + 1) * 128],
                     wtmp[:, rc * 128:(rc + 1) * 128], id_bf[:, :],
                     inc=("spe", 1) if rc == 3 else None)
            p.wait("vector", "spe", None)
            p.op("vector", "tensor_copy",
                 wout2T[:, :].rearrange("p (c n) -> p c n", n=1024)[:, :, oc * 128:(oc + 1) * 128],
                 pT[:, 0:512].rearrange("p (c n) -> p c n", n=128), inc=("sdv", 1))
        p.wait("tensor", "sdv", None)

    with p.block("d0"):
        for t in range(NTILE):
            xb = xbuf[t % 2]
            ab_ = abuf[t % 2]
            x3b = x3buf[t % 2]
            p.wait("sync", "sst", None)
            p.wait("sync", "sdv", None)
            p.op("sync", "dma_start", xb[:, :], x2_d[t * 128:(t + 1) * 128, :],
                 inc=("sld", 16))
            p.op("sync", "dma_start", ab_[:, :], ac_d[t * 128:(t + 1) * 128, :],
                 inc=("sld", 16))
            p.wait("scalar", "sld", None)
            p.wait("vector", "sld", None)
            emit_stats_d(ab_[:, :], scr[:, :])
            p.op("vector", "tensor_scalar", scr[:, :], ab_[:, :], smean[:, :], srstd[:, :],
                 ALU.subtract, ALU.mult, edge=True)
            p.op("vector", "tensor_tensor", x3b[:, :], scr[:, :], xb[:, :], ALU.add,
                 inc=("sdv", 1))
            p.wait("sync", "sdv", None)
            p.op("sync", "dma_start", x3_dram[t * 128:(t + 1) * 128, :], x3b[:, :],
                 inc=("sst", 16))
            p.wait("scalar", "sdv", None)
            emit_stats_d(x3b[:, :], scr[:, :])
            p.op("vector", "tensor_scalar", xln_bf[:, :], x3b[:, :], smean[:, :],
                 srstd[:, :], ALU.subtract, ALU.mult, inc=("sdv", 1))
            p.wait("tensor", "sdv", None)
            p.wait("tensor", "sac", None)
            for half in range(2):
                for c4 in range(4):
                    c = half * 4 + c4
                    p.op("tensor", "transpose",
                         pT[:, half * 512 + c4 * 128: half * 512 + (c4 + 1) * 128],
                         xln_bf[:, c * 128:(c + 1) * 128], id_bf[:, :],
                         inc=("spe", 1) if c4 == 3 else None)
                p.wait("vector", "spe", None)
                p.op("vector", "tensor_copy",
                     xlnT[:, :].rearrange("p (c n) -> p c n", n=NT)
                         [:, half * 4:(half + 1) * 4, t * 128:(t + 1) * 128],
                     pT[:, half * 512:(half + 1) * 512].rearrange("p (c n) -> p c n", n=128),
                     inc=("sdv", 1))
                p.wait("tensor", "sdv", None)

    with p.block("d2"):
        for t in range(NTILE):
            p.wait("tensor", "sac", None)
            p.wait("tensor", "sdv", None)
            for c in range(8):
                p.op("tensor", "matmul", pZ[:, 0:16],
                     xlnT[:, :].rearrange("p (c n) -> p c n", n=NT)[:, c, t * 128:(t + 1) * 128],
                     projsb[:, c * 16:(c + 1) * 16], start=(c == 0), stop=(c == 7),
                     inc=("spe", 1) if c == 7 else None)
            p.wait("vector", "spe", None)
            p.op("vector", "tensor_reduce", ssum[:, :], pZ[:, 0:16], AX.X, ALU.add,
                 inc=("sdv", 1))
            p.wait("scalar", "spe", None)
            p.op("scalar", "activation", dmp16[:, :], pZ[:, 0:16], AF.Square,
                 accum_out=ssq[:, :], inc=("sac", 1))
            p.wait("vector", "sac", None)
            p.op("vector", "tensor_scalar", smean[:, :], ssum[:, :], 1.0 / 16, None,
                 ALU.mult, edge=True)
            p.op("vector", "tensor_tensor", msq[:, :], smean[:, :], smean[:, :], ALU.mult)
            p.op("vector", "tensor_scalar", ssq[:, :], ssq[:, :], 1.0 / 16, None, ALU.mult,
                 edge=True)
            p.op("vector", "tensor_tensor", srstd[:, :], ssq[:, :], msq[:, :], ALU.subtract,
                 inc=("sdv", 1))
            p.wait("scalar", "sdv", None)
            p.op("scalar", "activation", srstd[:, :], srstd[:, :], AF.Ln, bias=epsc[:, :],
                 edge=True)
            p.op("scalar", "activation", srstd[:, :], srstd[:, :], AF.Exp, scale=-0.5,
                 inc=("sac", 1))
            p.wait("vector", "sac", None)
            p.op("vector", "tensor_scalar", srstd[:, :], srstd[:, :], 0.5, None, ALU.mult,
                 edge=True)
            p.op("vector", "tensor_scalar", zt[:, :], pZ[:, 0:16], smean[:, :], srstd[:, :],
                 ALU.subtract, ALU.mult, edge=True)
            p.op("vector", "tensor_reduce", nmax[:, :], zt[:, :], AX.X, ALU.max,
                 negate=True, edge=True)
            p.op("vector", "tensor_scalar", zt[:, :], zt[:, :], nmax[:, :], None, ALU.add,
                 edge=True)
            p.op("vector", "memset", losA[:, :], -1.0)
            p.op("vector", "memset", hisA[:, :], 0.0, inc=("sdv", 1))
            p.wait("vector", "sdv", None)
            lo, lo2, hi, hi2 = losA, losB, hisA, hisB
            for it in range(NBIS):
                p.op("vector", "tensor_tensor", mid[:, :], lo[:, :], hi[:, :], ALU.add,
                     edge=True)
                p.op("vector", "tensor_scalar", mid[:, :], mid[:, :], 0.5, None, ALU.mult,
                     edge=True)
                p.op("vector", "tensor_scalar", rl[:, :], zt[:, :], mid[:, :], 0.0,
                     ALU.subtract, ALU.max, edge=True)
                p.op("vector", "scalar_tensor_tensor", dmp16[:, :], rl[:, :], 1.0, rl[:, :],
                     ALU.bypass, ALU.mult, accum_out=gac[:, :], edge=True)
                p.op("vector", "tensor_scalar", mk_[:, :], gac[:, :], 1.0, None, ALU.is_gt,
                     edge=True)
                p.op("vector", "tensor_tensor", dd[:, :], mid[:, :], lo[:, :], ALU.subtract,
                     edge=True)
                p.op("vector", "scalar_tensor_tensor", lo2[:, :], dd[:, :], mk_[:, :],
                     lo[:, :], ALU.mult, ALU.add, edge=True)
                p.op("vector", "tensor_tensor", dd[:, :], hi[:, :], mid[:, :], ALU.subtract,
                     edge=True)
                p.op("vector", "scalar_tensor_tensor", hi2[:, :], dd[:, :], mk_[:, :],
                     mid[:, :], ALU.mult, ALU.add, edge=True)
                lo, lo2 = lo2, lo
                hi, hi2 = hi2, hi
            tau, tau2 = lo, lo2
            for it in range(NNEWT):
                p.op("vector", "tensor_scalar", rl[:, :], zt[:, :], tau[:, :], 0.0,
                     ALU.subtract, ALU.max, edge=True)
                p.op("vector", "scalar_tensor_tensor", dmp16[:, :], rl[:, :], 1.0, rl[:, :],
                     ALU.bypass, ALU.mult, accum_out=gac[:, :], edge=True)
                p.op("vector", "tensor_scalar", dmp16[:, :], rl[:, :], 1.0, 0.0, ALU.mult,
                     ALU.add, accum_out=s1c[:, :], edge=True)
                p.op("vector", "reciprocal", rec[:, :], s1c[:, :], edge=True)
                p.op("vector", "tensor_scalar", gac[:, :], gac[:, :], 1.0, None,
                     ALU.subtract, edge=True)
                p.op("vector", "tensor_tensor", dd[:, :], gac[:, :], rec[:, :], ALU.mult,
                     edge=True)
                p.op("vector", "tensor_scalar", dd[:, :], dd[:, :], 0.5, None, ALU.mult,
                     edge=True)
                p.op("vector", "tensor_tensor", tau2[:, :], tau[:, :], dd[:, :], ALU.add,
                     edge=True)
                tau, tau2 = tau2, tau
            p.op("vector", "tensor_scalar", rl[:, :], zt[:, :], tau[:, :], 0.0,
                 ALU.subtract, ALU.max, edge=True)
            p.op("vector", "tensor_tensor", dmp16[:, :], rl[:, :], rl[:, :], ALU.mult,
                 edge=True)
            p.op("vector", "tensor_copy", ab[:, :], dmp16[:, :], inc=("sdv", 1))
            p.wait("tensor", "sdv", None)
            p.op("tensor", "transpose", pT[0:16, 0:128], ab[:, :], id_bf[:, :],
                 inc=("spe", 1))
            p.wait("vector", "spe", None)
            p.op("vector", "tensor_copy", aT[:, t * 128:(t + 1) * 128], pT[0:16, 0:128],
                 inc=("sdv", 1))
            p.wait("tensor", "sdv", None)

    with p.block("dmain"):
        for g in range(NG):
            for gi, (cbf, gT) in enumerate(((c1bf, g1T), (c2bf, g2T))):
                for rc in range(4):
                    p.wait("tensor", "sac", None)
                    p.wait("tensor", "sdv", None)
                    p.op("tensor", "matmul", pZ[:, 0:512], cbf[:, rc * 128:(rc + 1) * 128],
                         aT[:, g * 512:(g + 1) * 512],
                         start=True, stop=True, inc=("spe", 1))
                    p.wait("scalar", "spe", None)
                    p.op("scalar", "activation",
                         gT[:, rc * 512:(rc + 1) * 512],
                         pZ[:, 0:512], AF.Copy, bias=0.0, inc=("sac", 1))
            for tt in range(1):
                tok0 = g * 512
                for rc in range(4):
                    p.wait("tensor", "sac", None)
                    p.wait("tensor", "sdv", None)
                    for dc in range(8):
                        p.op("tensor", "matmul", pZ[:, 0:512],
                             win1sb[:, dc * 512 + rc * 128: dc * 512 + (rc + 1) * 128],
                             xlnT[:, :].rearrange("p (c n) -> p c n", n=NT)[:, dc, tok0:tok0 + 512],
                             start=(dc == 0), stop=(dc == 7),
                             inc=("spe", 1) if dc == 7 else None)
                    p.wait("vector", "spe", None)
                    p.op("vector", "scalar_tensor_tensor", zg1[:, rc * 512:(rc + 1) * 512],
                         pZ[:, 0:512], bz1[:, rc:rc + 1],
                         g1T[:, rc * 512:(rc + 1) * 512],
                         ALU.add, ALU.mult, inc=("sdv", 1))
                p.wait("tensor", "sdv", None)
                for oc in range(32):
                    ph = pH2[:, (oc % 2) * 512:(oc % 2 + 1) * 512]
                    p.wait("tensor", "sdv", None)
                    for rc in range(4):
                        p.op("tensor", "matmul", ph,
                             woutT[:, rc * 4096 + oc * 128: rc * 4096 + (oc + 1) * 128],
                             zg1[:, rc * 512:(rc + 1) * 512],
                             start=(rc == 0), stop=(rc == 3),
                             inc=("spe", 1) if rc == 3 else None)
                    p.wait("scalar", "spe", None)
                    p.op("scalar", "activation", hb[:, :], ph, AF.Gelu, bias=0.0,
                         inc=("sac", 1))
                    p.wait("tensor", "sac", None)
                    for r2c in range(4):
                        pp = pA if r2c < 2 else pB
                        p.op("tensor", "matmul", pp[:, (r2c % 2) * 512:(r2c % 2 + 1) * 512],
                             win2sb[:, oc * 512 + r2c * 128: oc * 512 + (r2c + 1) * 128],
                             hb[:, :], start=(oc == 0), stop=(oc == 31),
                             inc=("spe", 1) if (oc == 31 and r2c == 3) else None)
                p.wait("vector", "spe", None)
                for r2c in range(4):
                    pp = pA if r2c < 2 else pB
                    p.op("vector", "scalar_tensor_tensor", zg2[:, r2c * 512:(r2c + 1) * 512],
                         pp[:, (r2c % 2) * 512:(r2c % 2 + 1) * 512], bz2[:, r2c:r2c + 1],
                         g2T[:, r2c * 512:(r2c + 1) * 512],
                         ALU.add, ALU.mult, inc=("sdv", 1) if r2c == 3 else None)
                p.wait("tensor", "sdv", None)
                for t4 in range(4):
                    gt = (tok0 // 128) + t4
                    x3b = x3buf[t4 % 2]
                    ob = obuf[t4 % 2]
                    p.wait("sync", "sst", None)
                    p.op("sync", "dma_start", x3b[:, :],
                         x3_dram[gt * 128:(gt + 1) * 128, :], inc=("sld", 16))
                    p.wait("tensor", "sac", None)
                    p.wait("tensor", "sdv", None)
                    for half in range(2):
                        for r2c in range(4):
                            p.op("tensor", "matmul",
                                 pH2[:, half * 512:(half + 1) * 512],
                                 zg2[:, r2c * 512 + t4 * 128: r2c * 512 + (t4 + 1) * 128],
                                 wout2T[:, r2c * 1024 + half * 512: r2c * 1024 + (half + 1) * 512],
                                 start=(r2c == 0), stop=(r2c == 3),
                                 inc=("spe", 1) if (half == 1 and r2c == 3) else None)
                    p.wait("scalar", "spe", None)
                    p.wait("scalar", "sld", None)
                    emit_stats_d(pH2[:, :], scr[:, :])
                    p.op("vector", "tensor_scalar", scr[:, :], pH2[:, :], smean[:, :],
                         srstd[:, :], ALU.subtract, ALU.mult, edge=True)
                    p.op("vector", "tensor_tensor", ob[:, :], scr[:, :], x3b[:, :], ALU.add,
                         inc=("sdv", 1))
                    p.wait("sync", "sdv", None)
                    p.op("sync", "dma_start", o[gt * 128:(gt + 1) * 128, :], ob[:, :],
                         inc=("sst", 16))
        p.wait("sync", "sst", None)

    st.close()
    return nc



def kernel(**inputs):
    w = inputs
    x = np.ascontiguousarray(w["L0_attn_x"], dtype=np.float32).reshape(32768, 1024)
    ncores = 8
    NT = 4096

    nca = build_a()
    in_maps = [inputs_a(x[c * NT:(c + 1) * NT], w) for c in range(ncores)]
    res = run_bass_kernel_spmd(nca, in_maps, core_ids=list(range(ncores)))
    x1 = np.concatenate([np.asarray(r["y"]) for r in res.results], axis=0)

    ncb = build_b()
    in_maps = [inputs_b(x1[c * NT:(c + 1) * NT], w) for c in range(ncores)]
    res = run_bass_kernel_spmd(ncb, in_maps, core_ids=list(range(ncores)))
    x2 = np.concatenate([np.asarray(r["y"]) for r in res.results], axis=0)
    ln2 = np.concatenate([np.asarray(r["y2"]) for r in res.results], axis=0)

    ncc = build_c()
    ln2r = ln2.reshape(16, 2048, 1024)
    in_maps = [inputs_c(ln2r[:, :, 128 * c:128 * (c + 1)], w) for c in range(ncores)]
    res = run_bass_kernel_spmd(ncc, in_maps, core_ids=list(range(ncores)))
    attnc = np.stack([np.asarray(r["o"]).reshape(16, 2048, 128) for r in res.results],
                     axis=2).reshape(16, 2048, 1024).reshape(32768, 1024)

    ncd = build_d()
    in_maps = [inputs_d(x2[c * NT:(c + 1) * NT], attnc[c * NT:(c + 1) * NT], w)
               for c in range(ncores)]
    res = run_bass_kernel_spmd(ncd, in_maps, core_ids=list(range(ncores)))
    out = np.concatenate([np.asarray(r["o"]) for r in res.results], axis=0)
    return out.reshape(16, 2048, 1024).astype(np.float32)


# version 02
# speedup vs baseline: 1.0206x; 1.0206x over previous
import contextlib
import numpy as np
import ml_dtypes
import concourse.bass as bass
import concourse.mybir as mybir
from concourse.bass_utils import run_bass_kernel_spmd

F32 = mybir.dt.float32
BF16 = mybir.dt.bfloat16
AF = mybir.ActivationFunctionType
ALU = mybir.AluOpType
AX = mybir.AxisListType

ENGINES = ("sync", "scalar", "vector", "tensor", "gpsimd")
DBG_SKIP_HEADS = False
SAFE_BLOCKS = False
DBG_C_PHASES = 3
DBG_C_CUT = 9

class Prog:

    def __init__(self, nc, stack):
        self.nc = nc
        self.stack = stack
        self.sems = {}

    def sem(self, name):
        h = self.stack.enter_context(self.nc.semaphore(name))
        self.sems[name] = [h, 0]
        return name

    def val(self, name):
        return self.sems[name][1]

    def wait(self, eng, name, v=None):
        h, c = self.sems[name]
        t = c if v is None else v
        if t > 0:
            self.q[eng].append(("wait", h, t))

    EDGE_SEM = {"vector": "sdv", "scalar": "sac", "tensor": "spe", "gpsimd": "sio"}

    def op(self, eng, fname, *a, inc=None, edge=False, **kw):
        incs = []
        if inc is not None:
            if isinstance(inc, tuple):
                inc = [inc]
            for name, n in inc:
                rec = self.sems[name]
                rec[1] += n
                incs.append((rec[0], n))
        if edge:
            name = self.EDGE_SEM[eng]
            rec = self.sems[name]
            rec[1] += 1
            incs.append((rec[0], 1))
        self.q[eng].append(("op", fname, a, kw, incs))
        if edge:
            rec = self.sems[self.EDGE_SEM[eng]]
            self.q[eng].append(("wait", rec[0], rec[1]))

    def clear_sems_block(self):
        nums = [rec[0].num for rec in self.sems.values()]
        lo, hi = min(nums), max(nums) + 1
        with self.nc.Block("semclear") as blk:
            def body(eng):
                eng.dma_reset(range(lo, hi))
                eng.sem_clear(range(lo, hi))
            blk.gpsimd(body)

    @contextlib.contextmanager
    def block(self, name=None):
        self.q = {e: [] for e in ENGINES}
        yield
        qs = self.q
        self.q = None
        with self.nc.Block(name) as blk:
            for e in ENGINES:
                entries = qs[e]
                if not entries:
                    continue

                def body(eng, entries=entries):
                    for ent in entries:
                        if ent[0] == "wait":
                            eng.wait_ge(ent[1], ent[2])
                        else:
                            _, fname, a, kw, incs = ent
                            r = getattr(eng, fname)(*a, **kw)
                            for h, n in incs:
                                r = r.then_inc(h, n)
                getattr(blk, e)(body)


def make_consts_a():
    idn = np.eye(128, dtype=np.float32)
    c = {}
    c["cid_bf"] = idn.astype(ml_dtypes.bfloat16)
    c["cid_f32"] = idn.copy()
    masks = np.zeros((128, 4 * 512), dtype=np.float32)
    for r in range(4):
        ii = r * 128 + np.arange(128)[:, None]
        jj = np.arange(512)[None, :]
        masks[:, r * 512:(r + 1) * 512] = np.where(jj <= ii, 0.0, -1e30)
    c["cmasks"] = masks.astype(ml_dtypes.bfloat16)
    lio = np.zeros((2, 2048), dtype=np.float32)
    lio[0, :] = 1.0
    lio[1, :] = -np.arange(2048, dtype=np.float32)
    c["clio4"] = lio
    c["cones1_bf"] = np.ones((1, 128), dtype=ml_dtypes.bfloat16)
    c["cones1_f32"] = np.ones((1, 128), dtype=np.float32)
    c["ceps"] = np.full((128, 1), 1e-5, dtype=np.float32)
    start = 2.0 ** (-2.0 ** -(np.log2(16) - 3))
    slopes = start * start ** np.arange(16)
    c["cslopes"] = slopes.reshape(16, 1).astype(np.float32)
    return c


def inputs_a(x_shard, w):
    m = dict(make_consts_a())
    m["x"] = np.ascontiguousarray(x_shard, dtype=np.float32)
    m["wq"] = w["L0_attn_Wq"]
    m["wk"] = w["L0_attn_Wk"]
    m["wv"] = w["L0_attn_Wv"]
    m["wp"] = w["L0_attn_Wproj"]
    m["qrw"] = w["L0_attn_q_rms_w"].reshape(1, 64)
    m["krw"] = w["L0_attn_k_rms_w"].reshape(1, 64)
    m["omg"] = w["L0_attn_omega"].reshape(16, 1)
    m["u"] = w["L0_attn_u"].reshape(16, 64)
    return m




def build_full(NCORES=8, NSEQ=2, NTILE=16):
    nc = bass.Bass("TRN2", target_bir_lowering=False)
    nc.detect_race_conditions = False
    st = contextlib.ExitStack()
    p = Prog(nc, st)

    T = NTILE * 128
    NT = NSEQ * T
    NBATCH = NCORES * NSEQ

    x = nc.dram_tensor("x", [NT, 1024], F32, kind="ExternalInput")
    wq_d = nc.dram_tensor("wq", [1024, 1024], F32, kind="ExternalInput")
    wk_d = nc.dram_tensor("wk", [1024, 1024], F32, kind="ExternalInput")
    wv_d = nc.dram_tensor("wv", [1024, 1024], F32, kind="ExternalInput")
    wp_d = nc.dram_tensor("wp", [1024, 1024], F32, kind="ExternalInput")
    qrw_d = nc.dram_tensor("qrw", [1, 64], F32, kind="ExternalInput")
    krw_d = nc.dram_tensor("krw", [1, 64], F32, kind="ExternalInput")
    omg_d = nc.dram_tensor("omg", [16, 1], F32, kind="ExternalInput")
    u_d = nc.dram_tensor("u", [16, 64], F32, kind="ExternalInput")
    rw_d = nc.dram_tensor("rw", [1024, 8], F32, kind="ExternalInput")
    rb_d = nc.dram_tensor("rb", [1, 8], F32, kind="ExternalInput")
    w1_d = nc.dram_tensor("w1", [8, 1024, 4096], F32, kind="ExternalInput")
    b1_d = nc.dram_tensor("b1", [8, 4096], F32, kind="ExternalInput")
    w2_d = nc.dram_tensor("w2", [8, 4096, 1024], F32, kind="ExternalInput")
    b2_d = nc.dram_tensor("b2", [8, 1024], F32, kind="ExternalInput")
    wq1_d = nc.dram_tensor("wq1", [64, 64], F32, kind="ExternalInput")
    wk1_d = nc.dram_tensor("wk1", [64, 64], F32, kind="ExternalInput")
    wv1_d = nc.dram_tensor("wv1", [64, 64], F32, kind="ExternalInput")
    proj_d = nc.dram_tensor("proj", [1024, 16], F32, kind="ExternalInput")
    c1_d = nc.dram_tensor("c1", [16, 512], F32, kind="ExternalInput")
    c2_d = nc.dram_tensor("c2", [16, 512], F32, kind="ExternalInput")
    win1_d = nc.dram_tensor("win1", [1025, 512], F32, kind="ExternalInput")
    wout1_d = nc.dram_tensor("wout1", [4096, 512], F32, kind="ExternalInput")
    win2_d = nc.dram_tensor("win2", [4097, 512], F32, kind="ExternalInput")
    wout2_d = nc.dram_tensor("wout2", [1024, 512], F32, kind="ExternalInput")
    id_bf_d = nc.dram_tensor("cid_bf", [128, 128], BF16, kind="ExternalInput")
    id_f32_d = nc.dram_tensor("cid_f32", [128, 128], F32, kind="ExternalInput")
    masks_d = nc.dram_tensor("cmasks", [128, 2048], BF16, kind="ExternalInput")
    lio4_d = nc.dram_tensor("clio4", [2, 2048], F32, kind="ExternalInput")
    ones1bf_d = nc.dram_tensor("cones1_bf", [1, 128], BF16, kind="ExternalInput")
    ones1f_d = nc.dram_tensor("cones1_f32", [1, 128], F32, kind="ExternalInput")
    eps_d = nc.dram_tensor("ceps", [128, 1], F32, kind="ExternalInput")
    slopes_d = nc.dram_tensor("cslopes", [16, 1], F32, kind="ExternalInput")
    ones_bf_d = nc.dram_tensor("cones_bf", [128, 128], BF16, kind="ExternalInput")
    out_d = nc.dram_tensor("out", [NT, 1024], F32, kind="ExternalOutput")
    x1d = nc.dram_tensor("x1d", [NT, 1024], F32)
    x2d = nc.dram_tensor("x2d", [NT, 1024], F32)
    ln2d = nc.dram_tensor("ln2d", [NT, 1024], F32)
    acd = nc.dram_tensor("acd", [NT, 1024], F32)
    fout = nc.dram_tensor("fout", [NT, 1024], F32)
    x3_dram = nc.dram_tensor("x3_dram", [NT, 1024], F32)
    pairs_dram = nc.dram_tensor("pairs_dram", [2, 16, T], F32)
    bounce2 = nc.dram_tensor("bounce2", [16, 64], BF16)
    cc_in = nc.dram_tensor("cc_in", [NSEQ * T, 16], F32)
    cc_out = nc.dram_tensor("cc_out", [NBATCH * T, 16], F32)

    for s in ("sld", "sldg", "sio", "sdv", "sac", "spe", "sst", "sfo", "sbar", "scc"):
        p.sem(s)
    p.clear_sems_block()
    dbg = ()

    y = x1d
    stA = contextlib.ExitStack()
    sb = lambda name, shape, dt: stA.enter_context(nc.sbuf_tensor("A" + name, shape, dt))
    psd = lambda name, shape, dt: stA.enter_context(nc.psum_tensor("A" + name, shape, dt))
    id_bf = sb("id_bf", [128, 128], BF16)
    id_f32 = sb("id_f32", [128, 128], F32)
    masks = sb("masks", [128, 2048], BF16)
    lio4 = sb("lio4", [2, 2048], F32)
    ones1bf = sb("ones1bf", [1, 128], BF16)
    ones1f = sb("ones1f", [1, 128], F32)
    epsc = sb("epsc", [128, 1], F32)

    wbuf = sb("wbuf", [128, 2 * 8192], BF16)
    KT = sb("KT", [128, 8 * T], BF16)
    Vtm = sb("Vtm", [128, NTILE * 1024], BF16)
    urep = sb("urep", [128, 1024], BF16)
    qwrep = sb("qwrep", [128, 64], F32)
    kwrep = sb("kwrep", [128, 64], F32)
    u16 = sb("u16", [16, 64], F32)
    udmp = sb("udmp", [16, 64], F32)
    u16bf = sb("u16bf", [16, 64], BF16)
    uss16 = sb("uss16", [16, 1], F32)
    unrm = sb("unrm", [16, 1], F32)
    urin = sb("urin", [16, 1], F32)
    urow_bf = sb("urow_bf", [1, 1024], BF16)
    wrow = sb("wrow", [1, 64], F32)
    om16 = sb("om16", [16, 1], F32)
    omsp = sb("omsp", [16, 1], F32)
    om8col = sb("om8col", [16, 1], F32)
    slopes16 = sb("slopes16", [16, 1], F32)

    xbuf = [sb(f"xbuf{i}", [128, 1024], F32) for i in range(2)]
    xln_bf = sb("xln_bf", [128, 1024], BF16)
    xlt = sb("xlt", [128, 1024], BF16)
    scr = sb("scr", [128, max(2048, T)], F32)
    sc = scr
    probs = sb("probs", [128, max(1024, T)], BF16)
    ptT = sb("ptT", [128, max(1024, T)], BF16)
    kqn = sb("kqn", [128, 1024], BF16)
    y_sb = sb("y_sb", [128, 1024], BF16)
    yT = sb("yT", [128, 1024], BF16)
    jW8 = sb("jW8", [16, T], F32)
    Wg = sb("Wg", [16, T], F32)
    pairsH = [sb(f"pairsH{i}", [2, T], F32) for i in range(4)]
    qs = [sb(f"qs{i}", [128, 128], BF16) for i in range(2)]
    ssum = sb("ssum", [128, 1], F32)
    ssq = sb("ssq", [128, 1], F32)
    smean = sb("smean", [128, 1], F32)
    srstd = sb("srstd", [128, 1], F32)
    msq = sb("msq", [128, 1], F32)
    hss = sb("hss", [128, 16], F32)
    hrstd = sb("hrstd", [128, 16], F32)
    glt = sb("glt", [128, 16], F32)
    gtmp = sb("gtmp", [128, 16], F32)
    gtmp2 = sb("gtmp2", [128, 16], F32)
    gtmp3 = sb("gtmp3", [128, 16], F32)
    nmax = sb("nmax", [128, 1], F32)
    rsum = sb("rsum", [128, 1], F32)
    rinv = sb("rinv", [128, 1], F32)
    uss = sb("uss", [1, 16], F32)
    udump = sb("udump", [1, 64], F32)

    pA = psd("pA", [128, 1024], F32)
    pB = psd("pB", [128, 1024], F32)
    pC = psd("pC", [128, 512], F32)
    pD = psd("pD", [128, 512], F32)
    pT = psd("pT", [128, 2048], BF16)

    def barrier():
        for e in ("scalar", "vector", "tensor", "gpsimd"):
            p.op(e, "drain", inc=("sbar", 1))
        p.op("sync", "nop", inc=("sbar", 1))
        tgt = p.val("sbar")
        for e in ("scalar", "vector", "tensor", "gpsimd", "sync"):
            p.wait(e, "sbar", tgt)

    with p.block("setup"):
        for dst, src in ((id_bf, id_bf_d), (id_f32, id_f32_d), (masks, masks_d),
                         (lio4, lio4_d), (ones1bf, ones1bf_d), (ones1f, ones1f_d),
                         (epsc, eps_d), (slopes16, slopes_d), (u16, u_d), (om16, omg_d)):
            p.op("sync", "dma_start", dst[:, :], src[:, :], inc=("sld", 16))
        for slot, wsrc in ((0, wk_d), (1, wv_d)):
            p.op("gpsimd", "dma_start",
                 wbuf[:, slot * 8192:(slot + 1) * 8192].rearrange("p (c n) -> p c n", c=8),
                 wsrc.ap().rearrange("(c p) n -> p c n", p=128), inc=("sldg", 16))
        nld = p.val("sld")
        p.wait("vector", "sld", nld)
        p.op("vector", "scalar_tensor_tensor", udmp[:, :], u16[:, :], 1.0, u16[:, :],
             ALU.bypass, ALU.mult, accum_out=uss16[:, :], inc=("sdv", 1))
        p.wait("scalar", "sdv", None)
        p.op("scalar", "activation", unrm[:, :], uss16[:, :], AF.Ln, edge=True)
        p.op("scalar", "activation", unrm[:, :], unrm[:, :], AF.Exp, scale=0.5,
             inc=("sac", 1))
        p.wait("vector", "sac", None)
        p.op("vector", "tensor_scalar", unrm[:, :], unrm[:, :], 1e-6, None, ALU.max,
             edge=True)
        p.op("vector", "reciprocal", urin[:, :], unrm[:, :], edge=True)
        p.op("vector", "tensor_scalar", u16[:, :], u16[:, :], urin[:, :], None, ALU.mult,
             edge=True)
        p.op("vector", "tensor_copy", u16bf[:, :], u16[:, :], inc=("sdv", 1))
        p.wait("sync", "sdv", None)
        p.op("sync", "dma_start", bounce2[:, :], u16bf[:, :], inc=("sld", 16))
        p.wait("sync", "sld", None)
        p.op("sync", "dma_start", urow_bf[:, :],
             bounce2.ap().rearrange("a b -> (a b)"), inc=("sld", 16))
        p.wait("scalar", "sld", None)
        p.op("scalar", "activation", omsp[:, :], om16[:, :], AF.Exp, edge=True)
        p.op("scalar", "activation", omsp[:, :], omsp[:, :], AF.Ln, bias=1.0,
             inc=("sac", 1))
        p.wait("vector", "sac", None)
        p.op("vector", "tensor_tensor", om8col[:, :], omsp[:, :], slopes16[:, :], ALU.mult,
             edge=True)
        p.op("vector", "tensor_scalar", om8col[:, :], om8col[:, :], 8.0, None, ALU.mult,
             inc=("sdv", 1))
        p.wait("tensor", "sld", None)
        p.wait("tensor", "sldg", None)
        p.op("tensor", "matmul", pA[:, 0:512], ones1bf[:, :], urow_bf[:, 0:512],
             start=True, stop=True)
        p.op("tensor", "matmul", pA[:, 512:1024], ones1bf[:, :], urow_bf[:, 512:1024],
             start=True, stop=True, inc=("spe", 1))
        p.wait("vector", "spe", None)
        p.op("vector", "tensor_copy", urep[:, 0:512], pA[:, 0:512])
        p.op("vector", "tensor_copy", urep[:, 512:1024], pA[:, 512:1024], inc=("sdv", 1))
        for dsrc, dst in ((qrw_d, qwrep), (krw_d, kwrep)):
            p.wait("sync", "spe", None)
            p.op("sync", "dma_start", wrow[:, :], dsrc[:, :], inc=("sld", 16))
            p.wait("tensor", "sld", None)
            p.wait("tensor", "sdv", None)
            p.op("tensor", "matmul", pA[:, 0:64], ones1f[:, :], wrow[:, :],
                 start=True, stop=True, inc=("spe", 1))
            p.wait("vector", "spe", None)
            p.op("vector", "tensor_copy", dst[:, :], pA[:, 0:64], inc=("sdv", 1))

    def emit_ln(xb, out_bf):
        p.op("scalar", "activation", xln_bf[:, :], xb[:, :], AF.Copy, bias=0.0,
             accum_out=ssum[:, :], inc=("sac", 1))
        p.op("vector", "scalar_tensor_tensor", scr[:, 0:1024], xb[:, :], 1.0, xb[:, :],
             ALU.bypass, ALU.mult, accum_out=ssq[:, :])
        p.wait("vector", "sac", None)
        p.op("vector", "tensor_scalar", smean[:, :], ssum[:, :], 1.0 / 1024, None, ALU.mult,
             edge=True)
        p.op("vector", "tensor_tensor", msq[:, :], smean[:, :], smean[:, :], ALU.mult)
        p.op("vector", "tensor_scalar", ssq[:, :], ssq[:, :], 1.0 / 1024, None, ALU.mult,
             edge=True)
        p.op("vector", "tensor_tensor", srstd[:, :], ssq[:, :], msq[:, :], ALU.subtract,
             inc=("sdv", 1))
        p.wait("scalar", "sdv", None)
        p.op("scalar", "activation", srstd[:, :], srstd[:, :], AF.Ln, bias=epsc[:, :],
             edge=True)
        p.op("scalar", "activation", srstd[:, :], srstd[:, :], AF.Exp, scale=-0.5,
             inc=("sac", 1))
        p.wait("vector", "sac", None)
        p.op("vector", "tensor_scalar", out_bf[:, :], xb[:, :], smean[:, :], srstd[:, :],
             ALU.subtract, ALU.mult, inc=("sdv", 1))

    def emit_transpose8(src_bf, dst):
        for half in range(2):
            p.wait("tensor", "sdv", None)
            for c in range(4):
                cc = half * 4 + c
                p.op("tensor", "transpose",
                     pT[:, half * 1024 + c * 128: half * 1024 + (c + 1) * 128],
                     src_bf[:, cc * 128:(cc + 1) * 128], id_bf[:, :],
                     inc=("spe", 1) if c == 3 else None)
            p.wait("vector", "spe", None)
            p.op("vector", "tensor_copy",
                 dst[:, half * 512:half * 512 + 512],
                 pT[:, half * 1024: half * 1024 + 512],
                 inc=("sdv", 1))

    def emit_proj(dst_ps, lhs, slot):
        for nh in range(2):
            for c in range(8):
                p.op("tensor", "matmul", dst_ps[:, nh * 512:(nh + 1) * 512],
                     lhs[:, c * 128:(c + 1) * 128],
                     wbuf[:, slot * 8192 + c * 1024 + nh * 512:
                          slot * 8192 + c * 1024 + (nh + 1) * 512],
                     start=(c == 0), stop=(c == 7),
                     inc=("spe", 1) if (c == 7) else None)

    def emit_rms(src_ps, wrep):
        p.wait("scalar", "spe", None)
        for h in range(16):
            p.op("scalar", "activation", scr[:, 1024 + h * 64:1024 + (h + 1) * 64],
                 src_ps[:, h * 64:(h + 1) * 64], AF.Square,
                 accum_out=hss[:, h:h + 1], inc=("sac", 1) if h == 15 else None)
        p.wait("vector", "sac", None)
        p.op("vector", "tensor_scalar", hrstd[:, :], hss[:, :], 1.0 / 64, None, ALU.mult,
             inc=("sdv", 1))
        p.wait("scalar", "sdv", None)
        p.op("scalar", "activation", hrstd[:, :], hrstd[:, :], AF.Ln, bias=epsc[:, :],
             edge=True)
        p.op("scalar", "activation", hrstd[:, :], hrstd[:, :], AF.Exp, scale=-0.5,
             inc=("sac", 1))
        p.wait("vector", "sac", None)
        for h in range(16):
            p.op("vector", "scalar_tensor_tensor", kqn[:, h * 64:(h + 1) * 64],
                 src_ps[:, h * 64:(h + 1) * 64], hrstd[:, h:h + 1], wrep[:, :],
                 ALU.mult, ALU.mult, inc=("sdv", 1) if h == 15 else None)

    pairs_ld = {}
    for s in range(NSEQ):
        with p.block(f"s{s}_p2"):
            if s > 0:
                p.wait("gpsimd", "spe", None)
                for slot, wsrc in ((0, wk_d), (1, wv_d)):
                    p.op("gpsimd", "dma_start",
                         wbuf[:, slot * 8192:(slot + 1) * 8192].rearrange("p (c n) -> p c n", c=8),
                         wsrc.ap().rearrange("(c p) n -> p c n", p=128), inc=("sldg", 16))
            wready = p.val("sldg")
            for t in range(NTILE):
                if SAFE_BLOCKS:
                    barrier()
                xb = xbuf[t % 2]
                p.wait("sync", "sdv", None)
                p.op("sync", "dma_start", xb[:, :],
                     x[s * T + t * 128: s * T + (t + 1) * 128, :], inc=("sld", 16))
                p.wait("scalar", "sld", None)
                p.wait("vector", "sld", None)
                emit_ln(xb, xln_bf)
                p.wait("tensor", "sdv", None)
                if t == 0:
                    p.wait("tensor", "sldg", wready)
                emit_transpose8(xln_bf, xlt)
                p.wait("tensor", "sdv", None)
                p.wait("tensor", "sac", None)
                emit_proj(pB, xlt, 0)
                p.wait("vector", "spe", None)
                emit_rms(pB, kwrep)
                for h in range(16):
                    p.op("vector", "scalar_tensor_tensor",
                         scr[:, 1024 + h * 64:1024 + (h + 1) * 64],
                         kqn[:, h * 64:(h + 1) * 64], 1.0, urep[:, h * 64:(h + 1) * 64],
                         ALU.bypass, ALU.mult, accum_out=glt[:, h:h + 1],
                         edge=(h == 15))
                p.op("vector", "tensor_scalar", glt[:, :], glt[:, :], 0.125, None, ALU.mult,
                     edge=True)
                p.op("vector", "tensor_scalar", gtmp[:, :], glt[:, :], -1.0, None, ALU.mult,
                     edge=True)
                p.op("vector", "tensor_tensor", gtmp3[:, :], gtmp[:, :], glt[:, :], ALU.max,
                     inc=("sdv", 1))
                p.wait("scalar", "sdv", None)
                p.op("scalar", "activation", gtmp[:, :], gtmp3[:, :], AF.Exp, scale=-1.0,
                     edge=True)
                p.op("scalar", "activation", gtmp[:, :], gtmp[:, :], AF.Ln, bias=1.0,
                     edge=True)
                p.op("scalar", "activation", gtmp2[:, :], glt[:, :], AF.Relu, inc=("sac", 1))
                p.wait("vector", "sac", None)
                p.op("vector", "tensor_tensor", gtmp[:, :], gtmp[:, :], gtmp2[:, :], ALU.add,
                     inc=("sdv", 1))
                p.wait("tensor", "sdv", None)
                p.op("tensor", "transpose", pD[0:16, 0:128], gtmp[:, :], id_f32[:, :],
                     inc=("spe", 1))
                p.wait("vector", "spe", None)
                p.op("vector", "tensor_copy", Wg[:, t * 128:(t + 1) * 128], pD[0:16, 0:128],
                     inc=("sdv", 1))
                p.wait("tensor", "sdv", None)
                for par in range(2):
                    for k in range(8):
                        h = 2 * k + par
                        p.op("tensor", "transpose",
                             pT[0:64, par * 1024 + k * 128: par * 1024 + (k + 1) * 128],
                             kqn[:, h * 64:(h + 1) * 64], id_bf[:, :],
                             inc=("spe", 1) if (par == 1 and k == 7) else None)
                p.wait("vector", "spe", None)
                for par in range(2):
                    dst = KT[64 * par:64 * par + 64, :].rearrange("q (c n) -> q c n", n=T)
                    src = pT[0:64, par * 1024:par * 1024 + 1024].rearrange("q (c n) -> q c n", n=128)
                    p.op("vector", "tensor_copy", dst[:, :, t * 128:(t + 1) * 128], src[:, :, :],
                         inc=("sdv", 1))
                p.wait("tensor", "sdv", None)
                emit_proj(pA, xlt, 1)
                p.wait("scalar", "spe", None)
                p.op("scalar", "activation", Vtm[:, t * 1024:t * 1024 + 512], pA[:, 0:512],
                     AF.Copy, bias=0.0)
                p.op("scalar", "activation", Vtm[:, t * 1024 + 512:(t + 1) * 1024],
                     pA[:, 512:1024], AF.Copy, bias=0.0, inc=("sac", 1))
            p.wait("vector", "sld", None)
            p.op("vector", "tensor_scalar", Wg[:, :], Wg[:, :], om8col[:, :], None, ALU.mult,
                 inc=("sdv", 1))
            iper = min(512, T)
            p.op("gpsimd", "iota", jW8[:, :], [[0, T // iper], [1, iper]], base=0,
                 channel_multiplier=0, allow_small_or_imprecise_dtypes=True,
                 inc=("sio", 1))
            p.wait("vector", "sio", None)
            p.op("vector", "tensor_tensor", jW8[:, :], jW8[:, :], Wg[:, :], ALU.mult,
                 inc=("sdv", 1))
            p.wait("sync", "sdv", None)
            p.op("sync", "dma_start", pairs_dram[0, :, :], jW8[:, :], inc=("sld", 16))
            p.op("sync", "dma_start", pairs_dram[1, :, :], Wg[:, :], inc=("sld", 16))

        with p.block(f"s{s}_p34"):
            p.wait("gpsimd", "spe", None)
            for slot, wsrc in ((0, wq_d), (1, wp_d)):
                p.op("gpsimd", "dma_start",
                     wbuf[:, slot * 8192:(slot + 1) * 8192].rearrange("p (c n) -> p c n", c=8),
                     wsrc.ap().rearrange("(c p) n -> p c n", p=128), inc=("sldg", 16))
            wready = p.val("sldg")
            pairs_ready = p.val("sld")
            for qi in range(NTILE):
                if SAFE_BLOCKS:
                    barrier()
                xb = xbuf[qi % 2]
                p.wait("sync", "sst", None)
                p.op("sync", "dma_start", xb[:, :],
                     x[s * T + qi * 128: s * T + (qi + 1) * 128, :], inc=("sld", 16))
                p.wait("scalar", "sld", None)
                p.wait("vector", "sld", None)
                emit_ln(xb, xln_bf)
                p.wait("tensor", "sdv", None)
                p.wait("tensor", "sac", None)
                if qi == 0:
                    p.wait("tensor", "sldg", wready)
                emit_transpose8(xln_bf, xlt)
                p.wait("tensor", "sdv", None)
                emit_proj(pA, xlt, 0)
                p.wait("vector", "spe", None)
                emit_rms(pA, qwrep)
                nj = qi // 4 + 1
                jlen = nj * 512
                for h in ([] if DBG_SKIP_HEADS else range(16)):
                    if SAFE_BLOCKS:
                        barrier()
                    par = h % 2
                    base = 64 * par
                    step = qi * 16 + h
                    pf = step + 3 if qi == 0 and h < 13 or True else None
                    if step == 0:
                        for s0 in range(3):
                            p.wait("sync", "spe", None)
                            p.op("sync", "dma_start", pairsH[s0 % 4][:, :],
                                 pairs_dram[:, s0 % 16, :], inc=("sld", 16))
                            pairs_ld[s0] = p.val("sld")
                    if step + 3 < 256:
                        p.wait("sync", "spe", None)
                        p.op("sync", "dma_start", pairsH[(step + 3) % 4][:, :],
                             pairs_dram[:, (step + 3) % 16, :], inc=("sld", 16))
                        pairs_ld[step + 3] = p.val("sld")
                    p.wait("tensor", "sdv", None)
                    p.wait("tensor", "sac", None)
                    p.op("tensor", "transpose", pT[0:64, 0:128], kqn[:, h * 64:(h + 1) * 64],
                         id_bf[:, :], inc=("spe", 1))
                    p.wait("vector", "spe", None)
                    p.op("vector", "tensor_copy", qs[h % 2][base:base + 64, :], pT[0:64, 0:128],
                         inc=("sdv", 1))
                    p.wait("tensor", "sdv", None)
                    p.wait("tensor", "sld", pairs_ld[step])
                    for jt in range(nj):
                        half = jt % 2
                        pbh = pB[:, half * 512:(half + 1) * 512]
                        p.wait("tensor", "sac", None)
                        p.op("tensor", "matmul", pbh, qs[h % 2][base:base + 64, :],
                             KT[base:base + 64, (h // 2) * T + jt * 512:
                                (h // 2) * T + (jt + 1) * 512],
                             start=True, stop=False)
                        p.op("tensor", "matmul", pbh,
                             lio4[:, qi * 128 - 512 * jt: qi * 128 - 512 * jt + 128],
                             pairsH[step % 4][:, jt * 512:(jt + 1) * 512],
                             start=False, stop=True, inc=("spe", 1))
                        p.wait("scalar", "spe", None)
                        p.op("scalar", "activation", sc[:, jt * 512:(jt + 1) * 512], pbh,
                             AF.Copy, bias=0.0, scale=0.125, inc=("sac", 1))
                    p.wait("vector", "sac", None)
                    p.op("vector", "tensor_tensor", sc[:, (nj - 1) * 512: jlen],
                         sc[:, (nj - 1) * 512: jlen], masks[:, (qi % 4) * 512:(qi % 4 + 1) * 512],
                         ALU.add)
                    p.op("vector", "tensor_reduce", nmax[:, :], sc[:, 0:jlen], AX.X, ALU.max,
                         negate=True, inc=("sdv", 1))
                    p.wait("scalar", "sdv", None)
                    p.op("scalar", "activation", probs[:, 0:jlen], sc[:, 0:jlen], AF.Exp,
                         bias=nmax[:, :], accum_out=rsum[:, :], inc=("sac", 1))
                    p.wait("vector", "sac", None)
                    p.op("vector", "reciprocal", rinv[:, :], rsum[:, :], inc=("sdv", 1))
                    ngrp = nj
                    for g in range(ngrp):
                        half = g % 2
                        p.wait("tensor", "sdv", None)
                        for c in range(4):
                            jc = g * 4 + c
                            p.op("tensor", "transpose",
                                 pT[:, half * 1024 + c * 128: half * 1024 + (c + 1) * 128],
                                 probs[:, jc * 128:(jc + 1) * 128], id_bf[:, :],
                                 inc=("spe", 1) if c == 3 else None)
                        p.wait("vector", "spe", None)
                        p.op("vector", "tensor_copy", ptT[:, g * 512:(g + 1) * 512],
                             pT[:, half * 1024:half * 1024 + 512], inc=("sdv", 1))
                    p.wait("tensor", "sdv", None)
                    nch = nj * 4
                    for jc in range(nch):
                        p.op("tensor", "matmul", pC[:, 0:64], ptT[:, jc * 128:(jc + 1) * 128],
                             Vtm[:, jc * 1024 + h * 64: jc * 1024 + (h + 1) * 64],
                             start=(jc == 0), stop=(jc == nch - 1),
                             inc=("spe", 1) if jc == nch - 1 else None)
                    p.wait("scalar", "spe", None)
                    p.wait("scalar", "sdv", None)
                    p.op("scalar", "activation", y_sb[:, h * 64:(h + 1) * 64], pC[:, 0:64],
                         AF.Copy, bias=0.0, scale=rinv[:, :], inc=("sac", 1))
                if DBG_SKIP_HEADS:
                    continue
                p.wait("tensor", "sac", None)
                emit_transpose8(y_sb, yT)
                p.wait("tensor", "sdv", None)
                emit_proj(pA, yT, 1)
                p.wait("scalar", "spe", None)
                p.wait("vector", "spe", None)
                p.op("scalar", "activation", probs[:, 0:1024], pA[:, :], AF.Copy, bias=0.0,
                     accum_out=ssum[:, :])
                p.op("scalar", "activation", scr[:, 0:1024], pA[:, :], AF.Square,
                     accum_out=ssq[:, :], inc=("sac", 1))
                p.wait("vector", "sac", None)
                p.op("vector", "tensor_scalar", smean[:, :], ssum[:, :], 1.0 / 1024, None,
                     ALU.mult, edge=True)
                p.op("vector", "tensor_tensor", msq[:, :], smean[:, :], smean[:, :], ALU.mult)
                p.op("vector", "tensor_scalar", ssq[:, :], ssq[:, :], 1.0 / 1024, None,
                     ALU.mult, edge=True)
                p.op("vector", "tensor_tensor", srstd[:, :], ssq[:, :], msq[:, :],
                     ALU.subtract, inc=("sdv", 1))
                p.wait("scalar", "sdv", None)
                p.op("scalar", "activation", srstd[:, :], srstd[:, :], AF.Ln, bias=epsc[:, :],
                     edge=True)
                p.op("scalar", "activation", srstd[:, :], srstd[:, :], AF.Exp, scale=-0.5,
                     inc=("sac", 1))
                p.wait("vector", "sac", None)
                p.op("vector", "tensor_scalar", scr[:, 1024:2048], pA[:, :], smean[:, :],
                     srstd[:, :], ALU.subtract, ALU.mult)
                p.op("vector", "tensor_tensor", xb[:, :], scr[:, 1024:2048],
                     xb[:, :], ALU.add, inc=("sdv", 1))
                p.wait("sync", "sdv", None)
                p.op("sync", "dma_start", y[s * T + qi * 128: s * T + (qi + 1) * 128, :],
                     xb[:, :], inc=("sst", 16))
            p.wait("sync", "sst", None)

    if dbg:
        dbg_tensors = dict(KT=KT, Vtm=Vtm, kqn=kqn, Wg=Wg, jW8=jW8, hrstd=hrstd,
                           scr=scr, probs=probs, ptT=ptT, y_sb=y_sb, yT=yT,
                           xlt=xlt, xln_bf=xln_bf, qwrep=qwrep, kwrep=kwrep,
                           urep=urep, om8col=om8col, rinv=rinv, glt=glt)
        with p.block("dbg"):
            for nm in dbg:
                t = dbg_tensors[nm]
                shp = list(t.shape)
                od = nc.dram_tensor(f"dbg_{nm}", shp, t.dtype, kind="ExternalOutput")
                p.op("sync", "dma_start", od[:, :], t[:, :], inc=("sst", 16))
            p.wait("sync", "sst", None)
    stA.close()

    x = x1d
    y = x2d
    y2 = ln2d
    NTILE_B = NSEQ * NTILE
    stB = contextlib.ExitStack()
    sb = lambda name, shape, dt: stB.enter_context(nc.sbuf_tensor("B" + name, shape, dt))
    psd = lambda name, shape, dt: stB.enter_context(nc.psum_tensor("B" + name, shape, dt))
    NTILE = NTILE_B
    NEXP = 8
    NTQ = NT // 512
    id_f32 = sb("id_f32", [128, 128], F32)
    epsc = sb("epsc", [128, 1], F32)
    ones1f = sb("ones1f", [1, 128], F32)
    rwsb = sb("rwsb", [128, 8 * 8], F32)
    rbrow = sb("rbrow", [1, 8], F32)
    brep = sb("brep", [128, 8], F32)
    b2nat = sb("b2nat", [8, 1024], F32)
    b1sb = sb("b1sb", [128, 32 * 8], F32)
    xlnT = sb("xlnT", [128, 8 * NT], BF16)
    gates_tm = sb("gates_tm", [128, NTILE * 8], F32)
    w1c = [sb(f"w1c{i}", [128, 8 * 128], BF16) for i in range(2)]
    xbuf = [sb(f"xbuf{i}", [128, 1024], F32) for i in range(2)]
    xln_f32 = sb("xln_f32", [128, 1024], F32)
    xlTf = sb("xlTf", [128, 1024], F32)
    scr = sb("scr", [128, 1024], F32)
    oscr = [sb(f"oscr{i}", [128, 1024], F32) for i in range(2)]
    fbuf = oscr
    ybuf = [sb(f"ybuf{i}", [128, 1024], F32) for i in range(2)]
    y2buf = [xln_f32, xlTf]
    lg = sb("lg", [128, 8], F32)
    lge = sb("lge", [128, 8], F32)
    gmask = sb("gmask", [128, 8], F32)
    ssum = sb("ssum", [128, 1], F32)
    ssq = sb("ssq", [128, 1], F32)
    smean = sb("smean", [128, 1], F32)
    srstd = sb("srstd", [128, 1], F32)
    msq = sb("msq", [128, 1], F32)
    nmax = sb("nmax", [128, 1], F32)
    rsum = sb("rsum", [128, 1], F32)
    rinv = sb("rinv", [128, 1], F32)

    pA = psd("pA", [128, 1024], F32)
    pB = psd("pB", [128, 1024], F32)
    pH = psd("pH", [128, 1024], F32)
    pS = psd("pS", [128, 512], F32)

    def emit_stats(src_ap, dump_bf):
        p.op("scalar", "activation", dump_bf, src_ap, AF.Copy, bias=0.0,
             accum_out=ssum[:, :])
        p.op("scalar", "activation", scr[:, :], src_ap, AF.Square,
             accum_out=ssq[:, :], inc=("sac", 1))
        p.wait("vector", "sac", None)
        p.op("vector", "tensor_scalar", smean[:, :], ssum[:, :], 1.0 / 1024, None, ALU.mult,
             edge=True)
        p.op("vector", "tensor_tensor", msq[:, :], smean[:, :], smean[:, :], ALU.mult)
        p.op("vector", "tensor_scalar", ssq[:, :], ssq[:, :], 1.0 / 1024, None, ALU.mult,
             edge=True)
        p.op("vector", "tensor_tensor", srstd[:, :], ssq[:, :], msq[:, :], ALU.subtract,
             inc=("sdv", 1))
        p.wait("scalar", "sdv", None)
        p.op("scalar", "activation", srstd[:, :], srstd[:, :], AF.Ln, bias=epsc[:, :],
             edge=True)
        p.op("scalar", "activation", srstd[:, :], srstd[:, :], AF.Exp, scale=-0.5,
             inc=("sac", 1))
        p.wait("vector", "sac", None)

    st2 = contextlib.ExitStack()
    sb2 = lambda name, shape, dt: st2.enter_context(nc.sbuf_tensor(name, shape, dt))
    b1nat = sb2("b1nat", [8, 4096], F32)
    gatesT = sb2("gatesT", [8, NT], F32)

    with p.block("bsetup"):
        for dst, srcd in ((id_f32, id_f32_d), (epsc, eps_d), (ones1f, ones1f_d),
                          (rbrow, rb_d), (b1nat, b1_d), (b2nat, b2_d)):
            p.op("sync", "dma_start", dst[:, :], srcd[:, :], inc=("sld", 16))
        p.op("sync", "dma_start", rwsb[:, :].rearrange("p (c e) -> p c e", e=8),
             rw_d.ap().rearrange("(c p) e -> p c e", p=128), inc=("sld", 16))
        p.wait("tensor", "sld", None)
        p.op("tensor", "matmul", pS[:, 0:8], ones1f[:, :], rbrow[:, :], start=True,
             stop=True, inc=("spe", 1))
        p.wait("vector", "spe", None)
        p.op("vector", "tensor_copy", brep[:, :], pS[:, 0:8], inc=("sdv", 1))
        for g in range(4):
            p.wait("tensor", "sdv", None)
            for k in range(8):
                fc = g * 8 + k
                p.op("tensor", "transpose", pH[:, k * 128:k * 128 + 8],
                     b1nat[:, fc * 128:(fc + 1) * 128], id_f32[0:8, 0:8],
                     inc=("spe", 1) if k == 7 else None)
            p.wait("vector", "spe", None)
            p.op("vector", "tensor_copy",
                 b1sb[:, g * 64:(g + 1) * 64].rearrange("p (c e) -> p c e", e=8),
                 pH[:, 0:1024].rearrange("p (c k) -> p c k", k=128)[:, :, 0:8],
                 inc=("sdv", 1))
        p.wait("tensor", "sdv", None)

    with p.block("b0"):
        for t in range(NTILE):
            xb = xbuf[t % 2]
            p.wait("sync", "sdv", None)
            p.op("sync", "dma_start", xb[:, :], x[t * 128:(t + 1) * 128, :], inc=("sld", 16))
            p.wait("scalar", "sld", None)
            p.wait("vector", "sld", None)
            emit_stats(xb[:, :], xln_f32[:, :])
            p.op("vector", "tensor_scalar", xln_f32[:, :], xb[:, :], smean[:, :],
                 srstd[:, :], ALU.subtract, ALU.mult, inc=("sdv", 1))
            p.wait("tensor", "sdv", None)
            for half in range(2):
                for c4 in range(4):
                    c = half * 4 + c4
                    p.op("tensor", "transpose", pH[:, half * 512 + c4 * 128: half * 512 + (c4 + 1) * 128],
                         xln_f32[:, c * 128:(c + 1) * 128], id_f32[:, :],
                         inc=("spe", 1) if c4 == 3 else None)
                p.wait("vector", "spe", None)
                p.op("vector", "tensor_copy", xlTf[:, half * 512:(half + 1) * 512],
                     pH[:, half * 512:(half + 1) * 512])
                p.op("vector", "tensor_copy",
                     xlnT[:, :].rearrange("p (c n) -> p c n", n=NT)[:, half * 4:(half + 1) * 4,
                                                                   t * 128:(t + 1) * 128],
                     pH[:, half * 512:(half + 1) * 512].rearrange("p (c n) -> p c n", n=128),
                     inc=("sdv", 1))
                p.wait("tensor", "sdv", None)
            for c in range(8):
                p.op("tensor", "matmul", pS[:, 0:8], xlTf[:, c * 128:(c + 1) * 128],
                     rwsb[:, c * 8:(c + 1) * 8], start=(c == 0), stop=(c == 7),
                     inc=("spe", 1) if c == 7 else None)
            p.wait("vector", "spe", None)
            p.op("vector", "tensor_tensor", lg[:, :], pS[:, 0:8], brep[:, :], ALU.add,
                 edge=True)
            p.op("vector", "tensor_reduce", nmax[:, :], lg[:, :], AX.X, ALU.max,
                 negate=True, inc=("sdv", 1))
            p.wait("scalar", "sdv", None)
            p.op("scalar", "activation", lge[:, :], lg[:, :], AF.Exp, bias=nmax[:, :],
                 accum_out=rsum[:, :], inc=("sac", 1))
            p.wait("vector", "sac", None)
            p.op("vector", "reciprocal", rinv[:, :], rsum[:, :], edge=True)
            p.op("vector", "tensor_scalar", gmask[:, :], lge[:, :], 1.0, None, ALU.is_ge,
                 edge=True)
            p.op("vector", "tensor_scalar", lge[:, :], lge[:, :], rinv[:, :], None, ALU.mult,
                 edge=True)
            p.op("vector", "tensor_tensor", gates_tm[:, t * 8:(t + 1) * 8], gmask[:, :],
                 lge[:, :], ALU.mult, inc=("sdv", 1))
            p.wait("tensor", "sdv", None)
            p.op("tensor", "transpose", pS[0:8, 128:256], gates_tm[:, t * 8:(t + 1) * 8],
                 id_f32[:, :], inc=("spe", 1))
            p.wait("vector", "spe", None)
            p.op("vector", "tensor_copy", gatesT[:, t * 128:(t + 1) * 128], pS[0:8, 128:256],
                 inc=("sdv", 1))
            p.wait("tensor", "sdv", None)
        for t in range(NTILE):
            pp = pA if t % 2 == 0 else pB
            osb = oscr[t % 2]
            p.wait("tensor", "sfo", None)
            for nh in range(2):
                p.op("tensor", "matmul", pp[:, nh * 512:(nh + 1) * 512],
                     gatesT[:, t * 128:(t + 1) * 128], b2nat[:, nh * 512:(nh + 1) * 512],
                     start=True, stop=True, inc=("spe", 1) if nh == 1 else None)
            p.wait("vector", "spe", None)
            p.wait("vector", "sfo", None)
            p.op("vector", "tensor_copy", osb[:, :], pp[:, :], inc=("sdv", 1))
            p.wait("gpsimd", "sdv", None)
            p.op("gpsimd", "dma_start", fout[t * 128:(t + 1) * 128, :], osb[:, :],
                 inc=("sfo", 16))

    st2.close()
    w2sb = sb("w2sb", [128, 32 * 1024], BF16)
    h_sb = sb("h_sb", [128, 32 * 512], BF16)

    with p.block("bmain"):
        for e in range(NEXP):
            p.wait("gpsimd", "spe", None)
            p.op("gpsimd", "dma_start",
                 w2sb[:, :].rearrange("p (c n) -> p c n", n=1024),
                 w2_d[e].rearrange("(c p) n -> p c n", p=128), inc=("sldg", 16))
            w2ready = p.val("sldg")
            for tq in range(NTQ):
                for fc in range(32):
                    wb = w1c[fc % 2]
                    p.wait("gpsimd", "spe", None)
                    p.op("gpsimd", "dma_start",
                         wb[:, :].rearrange("p (c n) -> p c n", n=128),
                         w1_d[e][:, fc * 128:(fc + 1) * 128].rearrange("(c p) n -> p c n", p=128),
                         inc=("sldg", 16))
                    wv = p.val("sldg")
                    p.wait("tensor", "sldg", wv)
                    p.wait("tensor", "sac", None)
                    ph = pH[:, (fc % 2) * 512:(fc % 2 + 1) * 512]
                    for dc in range(8):
                        p.op("tensor", "matmul", ph,
                             wb[:, dc * 128:(dc + 1) * 128],
                             xlnT[:, :].rearrange("p (c n) -> p c n", n=NT)
                                 [:, dc, tq * 512:(tq + 1) * 512],
                             start=(dc == 0), stop=(dc == 7),
                             inc=("spe", 1) if dc == 7 else None)
                    p.wait("scalar", "spe", None)
                    p.op("scalar", "activation",
                         h_sb[:, fc * 512:(fc + 1) * 512], ph, AF.Relu,
                         bias=b1sb[:, fc * 8 + e: fc * 8 + e + 1], inc=("sac", 1))
                if tq == 0:
                    p.wait("tensor", "sldg", w2ready)
                for tt in range(4):
                    gt = tq * 4 + tt
                    pp = pA if gt % 2 == 0 else pB
                    osb = oscr[gt % 2]
                    p.wait("tensor", "sdv", None)
                    p.wait("tensor", "sac", None)
                    for fc in range(32):
                        for nh in range(2):
                            p.op("tensor", "matmul",
                                 pp[:, nh * 512:(nh + 1) * 512],
                                 h_sb[:, fc * 512 + tt * 128: fc * 512 + (tt + 1) * 128],
                                 w2sb[:, fc * 1024 + nh * 512: fc * 1024 + (nh + 1) * 512],
                                 start=(fc == 0), stop=(fc == 31),
                                 inc=("spe", 1) if (fc == 31 and nh == 1) else None)
                    p.wait("vector", "spe", None)
                    p.wait("vector", "sfo", None)
                    p.op("vector", "tensor_scalar", osb[:, :], pp[:, :],
                         gates_tm[:, gt * 8 + e: gt * 8 + e + 1], None, ALU.mult,
                         inc=("sdv", 1))
                    p.wait("gpsimd", "sdv", None)
                    p.op("gpsimd", "dma_start", fout[gt * 128:(gt + 1) * 128, :],
                         osb[:, :], accum_op=ALU.add, inc=("sfo", 16))

    with p.block("bexit"):
        p.wait("sync", "sfo", None)
        for t in range(NTILE):
            xb = xbuf[t % 2]
            fb = fbuf[t % 2]
            yb = ybuf[t % 2]
            y2b = y2buf[t % 2]
            p.wait("sync", "sst", None)
            p.op("sync", "dma_start", xb[:, :], x[t * 128:(t + 1) * 128, :], inc=("sld", 16))
            p.op("sync", "dma_start", fb[:, :], fout[t * 128:(t + 1) * 128, :],
                 inc=("sld", 16))
            p.wait("scalar", "sld", None)
            p.wait("vector", "sld", None)
            emit_stats(fb[:, :], scr[:, :])
            p.op("vector", "tensor_scalar", scr[:, :], fb[:, :], smean[:, :], srstd[:, :],
                 ALU.subtract, ALU.mult, edge=True)
            p.op("vector", "tensor_tensor", yb[:, :], scr[:, :], xb[:, :], ALU.add,
                 inc=("sdv", 1))
            p.wait("scalar", "sdv", None)
            emit_stats(yb[:, :], scr[:, :])
            p.op("vector", "tensor_scalar", y2b[:, :], yb[:, :], smean[:, :], srstd[:, :],
                 ALU.subtract, ALU.mult, inc=("sdv", 1))
            p.wait("sync", "sdv", None)
            p.op("sync", "dma_start", y[t * 128:(t + 1) * 128, :], yb[:, :], inc=("sst", 16))
            p.op("sync", "dma_start", y2[t * 128:(t + 1) * 128, :], y2b[:, :], inc=("sst", 16))
        p.wait("sync", "sst", None)
    stB.close()

    NTILE = NTILE_B // NSEQ
    stC = contextlib.ExitStack()
    sb = lambda name, shape, dt: stC.enter_context(nc.sbuf_tensor("C" + name, shape, dt))
    psd = lambda name, shape, dt: stC.enter_context(nc.psum_tensor("C" + name, shape, dt))

    id_bf = sb("id_bf", [128, 128], BF16)
    ones_bf = sb("ones_bf", [128, 128], BF16)
    id_f32 = sb("id_f32", [128, 128], F32)
    wt = sb("wt", [64, 192], F32)
    wqT = sb("wqT", [128, 64], BF16)
    wkT = sb("wkT", [128, 64], BF16)
    wvT = sb("wvT", [128, 64], BF16)
    lnb = [sb(f"lnb{i}", [128, 1024], F32) for i in range(2)]
    xtb = sb("xtb", [128, 1024], BF16)
    xrT = sb("xrT", [128, 8 * 128], BF16)
    qsb = sb("qsb", [128, 1024], BF16)
    vsb_all = sb("vsb_all", [128, NTILE * 1024], BF16)
    DSs = sb("DSs", [128, NSEQ * NTILE * 16], F32)
    dcol = sb("dcol", [128, 64], F32)
    Bs_sb = sb("Bs_sb", [128, NSEQ * 1024], BF16)
    gbuf = [sb(f"gbuf{i}", [128, NBATCH * 16], F32) for i in range(2)]
    mxA = sb("mxA", [128, 16], F32)
    mxB = sb("mxB", [128, 16], F32)
    smA = sb("smA", [128, 16], F32)
    smB = sb("smB", [128, 16], F32)
    rin16 = sb("rin16", [128, 16], F32)
    pex = sb("pex", [128, 16], F32)
    prb = sb("prb", [128, 16], F32)
    obc = [sb(f"obc{i}", [128, 1024], F32) for i in range(2)]
    pQ = [psd("pQ0", [128, 512], F32), psd("pQ1", [128, 512], F32)]
    pK = [psd("pK0", [128, 512], F32), psd("pK1", [128, 512], F32)]
    pV = [psd("pV0", [128, 512], F32), psd("pV1", [128, 512], F32)]
    pTc = psd("pTc", [128, 1024], BF16)

    with p.block("cset"):
        for i, wsrc in enumerate((wq1_d, wk1_d, wv1_d)):
            p.op("sync", "dma_start", wt[:, i * 64:(i + 1) * 64], wsrc[:, :],
                 inc=("sld", 16))
        for dst, srcd in ((id_f32, id_f32_d), (id_bf, id_bf_d), (ones_bf, ones_bf_d)):
            p.op("sync", "dma_start", dst[:, :], srcd[:, :], inc=("sld", 16))
        p.wait("tensor", "sld", None)
        for i, wdst in enumerate((wqT, wkT, wvT)):
            p.op("tensor", "transpose", pQ[0][0:64, 0:64], wt[:, i * 64:(i + 1) * 64],
                 id_f32[0:64, 0:64], inc=("spe", 1))
            p.wait("vector", "spe", None)
            p.op("vector", "tensor_copy", wdst[0:64, :], pQ[0][0:64, 0:64])
            p.op("vector", "tensor_copy", wdst[64:128, :], pQ[0][0:64, 0:64], inc=("sdv", 1))
            p.wait("tensor", "sdv", None)

    with p.block("cpassA"):
        for s in range(NSEQ):
            for tt in range(NTILE):
                xb = lnb[tt % 2]
                p.wait("sync", "sdv", None)
                p.op("sync", "dma_start", xb[:, :],
                     ln2d[(s * T + tt * 128):(s * T + (tt + 1) * 128), :], inc=("sld", 16))
                p.wait("vector", "sld", None)
                p.op("vector", "tensor_copy", xtb[:, :], xb[:, :], inc=("sdv", 1))
                p.wait("tensor", "sdv", None)
                p.wait("tensor", "sac", None)
                for par in range(2):
                    for k in range(8):
                        h = 2 * k + par
                        p.op("tensor", "transpose",
                             pTc[0:64, par * 1024 + k * 128: par * 1024 + (k + 1) * 128]
                             if False else
                             pTc[0:64, k * 128:(k + 1) * 128],
                             xtb[:, h * 64:(h + 1) * 64], id_bf[:, :],
                             inc=("spe", 1) if k == 7 else None)
                    p.wait("vector", "spe", None)
                    p.op("vector", "tensor_copy", xrT[64 * par:64 * par + 64, :],
                         pTc[0:64, 0:1024], inc=("sdv", 1))
                    p.wait("tensor", "sdv", None)
                for h in range(16):
                    par = h % 2
                    b = 64 * par
                    cc = (h // 2) * 64
                    p.op("tensor", "matmul", pQ[par][:, cc:cc + 64],
                         xrT[b:b + 64, (h // 2) * 128:(h // 2) * 128 + 128],
                         wqT[b:b + 64, :], start=True, stop=True)
                    p.op("tensor", "matmul", pK[par][:, cc:cc + 64],
                         xrT[b:b + 64, (h // 2) * 128:(h // 2) * 128 + 128],
                         wkT[b:b + 64, :], start=True, stop=True)
                    p.op("tensor", "matmul", pV[par][:, cc:cc + 64],
                         xrT[b:b + 64, (h // 2) * 128:(h // 2) * 128 + 128],
                         wvT[b:b + 64, :], start=True, stop=True,
                         inc=("spe", 1) if h == 15 else None)
                p.wait("scalar", "spe", None)
                for h in range(16):
                    par = h % 2
                    cc = (h // 2) * 64
                    p.op("scalar", "activation", qsb[:, h * 64:(h + 1) * 64],
                         pQ[par][:, cc:cc + 64], AF.Copy, bias=0.0)
                    p.op("scalar", "activation",
                         vsb_all[:, tt * 1024 + h * 64: tt * 1024 + (h + 1) * 64],
                         pV[par][:, cc:cc + 64], AF.Copy, bias=0.0,
                         inc=("sac", 1) if h == 15 else None)
                p.wait("vector", "sac", None)
                for h in range(16):
                    par = h % 2
                    cc = (h // 2) * 64
                    p.op("vector", "scalar_tensor_tensor", dcol[:, :],
                         pK[par][:, cc:cc + 64], 1.0, qsb[:, h * 64:(h + 1) * 64],
                         ALU.bypass, ALU.mult,
                         accum_out=DSs[:, s * NTILE * 16 + tt * 16 + h:
                                       s * NTILE * 16 + tt * 16 + h + 1],
                         inc=("sdv", 1) if h == 15 else None)
            p.wait("tensor", "sac", None)
            p.wait("tensor", "sdv", None)
            for tt in range(NTILE):
                p.op("tensor", "matmul", pQ[0][:, 0:512], ones_bf[:, :],
                     vsb_all[:, tt * 1024: tt * 1024 + 512],
                     start=(tt == 0), stop=(tt == NTILE - 1))
                p.op("tensor", "matmul", pQ[1][:, 0:512], ones_bf[:, :],
                     vsb_all[:, tt * 1024 + 512:(tt + 1) * 1024],
                     start=(tt == 0), stop=(tt == NTILE - 1),
                     inc=("spe", 1) if tt == NTILE - 1 else None)
            p.wait("scalar", "spe", None)
            p.op("scalar", "activation", Bs_sb[:, s * 1024: s * 1024 + 512],
                 pQ[0][:, 0:512], AF.Copy, bias=0.0)
            p.op("scalar", "activation", Bs_sb[:, s * 1024 + 512:(s + 1) * 1024],
                 pQ[1][:, 0:512], AF.Copy, bias=0.0, inc=("sac", 1))
            p.wait("tensor", "sac", None)
        p.op("vector", "tensor_scalar", DSs[:, :], DSs[:, :], 1.0 / 32.0, None, ALU.mult,
             inc=("sdv", 1))
        p.wait("sync", "sdv", None)
        p.op("sync", "dma_start",
             bass.AP(cc_in, 0, [[16, 128], [T * 16, NSEQ], [128 * 16, NTILE], [1, 16]]),
             DSs[:, :].rearrange("q (s t h) -> q s t h", s=NSEQ, h=16), inc=("sld", 16))

    with p.block("ccoll"):
        p.wait("gpsimd", "sld", None)
        p.op("gpsimd", "collective_compute", "AllGather", ALU.bypass,
             replica_groups=[list(range(NCORES))],
             ins=[cc_in.ap().opt()], outs=[cc_out.ap().opt()], inc=("scc", 1))
        p.wait("gpsimd", "scc", None)

    with p.block("cpassB"):
        for tt in range(NTILE):
            gb = gbuf[tt % 2]
            p.wait("sync", "sdv", None)
            p.op("sync", "dma_start", gb[:, :],
                 bass.AP(cc_out, tt * 128 * 16,
                         [[16, 128], [T * 16, NBATCH], [1, 16]]),
                 inc=("sld", 16))
            p.wait("vector", "sld", None)
            mx, mx2 = mxA, mxB
            p.op("vector", "tensor_copy", mx[:, :], gb[:, 0:16], edge=True)
            for c in range(1, NBATCH):
                p.op("vector", "tensor_tensor", mx2[:, :], mx[:, :],
                     gb[:, c * 16:(c + 1) * 16], ALU.max, edge=True)
                mx, mx2 = mx2, mx
            sm, sm2 = smA, smB
            first = True
            for c in range(NBATCH):
                p.op("vector", "tensor_tensor", pex[:, :], gb[:, c * 16:(c + 1) * 16],
                     mx[:, :], ALU.subtract, inc=("sdv", 1))
                p.wait("scalar", "sdv", None)
                p.op("scalar", "activation", pex[:, :], pex[:, :], AF.Exp, inc=("sac", 1))
                p.wait("vector", "sac", None)
                if first:
                    p.op("vector", "tensor_copy", sm[:, :], pex[:, :], edge=True)
                    first = False
                else:
                    p.op("vector", "tensor_tensor", sm2[:, :], sm[:, :], pex[:, :],
                         ALU.add, edge=True)
                    sm, sm2 = sm2, sm
            p.op("vector", "reciprocal", rin16[:, :], sm[:, :], edge=True)
            for s in range(NSEQ):
                ob = obc[s % 2]
                p.op("vector", "tensor_tensor", pex[:, :],
                     DSs[:, s * NTILE * 16 + tt * 16: s * NTILE * 16 + (tt + 1) * 16],
                     mx[:, :], ALU.subtract, inc=("sdv", 1))
                p.wait("scalar", "sdv", None)
                p.op("scalar", "activation", pex[:, :], pex[:, :], AF.Exp, inc=("sac", 1))
                p.wait("vector", "sac", None)
                p.op("vector", "tensor_tensor", prb[:, :], pex[:, :], rin16[:, :],
                     ALU.mult, edge=True)
                p.wait("vector", "sst", None)
                for h in range(16):
                    p.op("vector", "tensor_scalar", ob[:, h * 64:(h + 1) * 64],
                         Bs_sb[:, s * 1024 + h * 64: s * 1024 + (h + 1) * 64],
                         prb[:, h:h + 1], None, ALU.mult,
                         inc=("sdv", 1) if h == 15 else None)
                p.wait("sync", "sdv", None)
                p.op("sync", "dma_start",
                     acd[(s * T + tt * 128):(s * T + (tt + 1) * 128), :], ob[:, :],
                     inc=("sst", 16))
        p.wait("sync", "sst", None)
    stC.close()

    x2_d = x2d
    ac_d = acd
    o = out_d
    NTILE = NTILE_B
    NBIS, NNEWT = 12, 3
    NG = NT // 512
    stD = contextlib.ExitStack()
    sb = lambda name, shape, dt: stD.enter_context(nc.sbuf_tensor("D" + name, shape, dt))
    psd = lambda name, shape, dt: stD.enter_context(nc.psum_tensor("D" + name, shape, dt))
    id_f32 = sb("id_f32", [128, 128], F32)
    id_bf = sb("id_bf", [128, 128], BF16)
    epsc = sb("epsc", [128, 1], F32)
    xlnT = sb("xlnT", [128, 8 * NT], BF16)
    win1sb = sb("win1sb", [128, 8 * 512], BF16)
    win2sb = sb("win2sb", [128, 32 * 512], BF16)
    woutT = sb("woutT", [128, 4 * 4096], BF16)
    wout2T = sb("wout2T", [128, 4 * 1024], BF16)
    projsb = sb("projsb", [128, 8 * 16], BF16)
    c1bf = sb("c1bf", [16, 512], BF16)
    c2bf = sb("c2bf", [16, 512], BF16)
    bz1 = sb("bz1", [128, 4], F32)
    bz2 = sb("bz2", [128, 4], F32)
    wrow = sb("wrow", [1, 512], F32)
    wtmp = sb("wtmp", [128, 512], BF16)
    aT = sb("aT", [16, NT], BF16)
    g1T = sb("g1T", [128, 4 * 512], BF16)
    g2T = sb("g2T", [128, 4 * 512], BF16)
    zg1 = sb("zg1", [128, 4 * 512], BF16)
    zg2 = zg1
    hb = sb("hb", [128, 512], BF16)
    xbuf = [sb(f"xbuf{i}", [128, 1024], F32) for i in range(2)]
    abuf = [sb(f"abuf{i}", [128, 1024], F32) for i in range(2)]
    x3buf = [sb(f"x3buf{i}", [128, 1024], F32) for i in range(2)]
    obuf = xbuf
    xln_bf = sb("xln_bf", [128, 1024], BF16)
    scr = sb("scr", [128, 1024], F32)
    ssum = sb("ssum", [128, 1], F32)
    ssq = sb("ssq", [128, 1], F32)
    smean = sb("smean", [128, 1], F32)
    srstd = sb("srstd", [128, 1], F32)
    msq = sb("msq", [128, 1], F32)
    zt = sb("zt", [128, 16], F32)
    rl = sb("rl", [128, 16], F32)
    dmp16 = sb("dmp16", [128, 16], F32)
    losA = sb("losA", [128, 1], F32)
    losB = sb("losB", [128, 1], F32)
    hisA = sb("hisA", [128, 1], F32)
    hisB = sb("hisB", [128, 1], F32)
    mid = sb("mid", [128, 1], F32)
    gac = sb("gac", [128, 1], F32)
    s1c = sb("s1c", [128, 1], F32)
    mk_ = sb("mk_", [128, 1], F32)
    dd = sb("dd", [128, 1], F32)
    rec = sb("rec", [128, 1], F32)
    nmax = sb("nmax", [128, 1], F32)
    ab = sb("ab", [128, 16], BF16)

    pA = psd("pA", [128, 1024], F32)
    pB = psd("pB", [128, 1024], F32)
    pH2 = psd("pH2", [128, 1024], F32)
    pZ = psd("pZ", [128, 512], F32)
    pT = psd("pT", [128, 1024], BF16)

    def emit_stats_d(src_ap, dump):
        p.op("scalar", "activation", dump, src_ap, AF.Copy, bias=0.0,
             accum_out=ssum[:, :])
        p.op("scalar", "activation", scr[:, :], src_ap, AF.Square,
             accum_out=ssq[:, :], inc=("sac", 1))
        p.wait("vector", "sac", None)
        p.op("vector", "tensor_scalar", smean[:, :], ssum[:, :], 1.0 / 1024, None, ALU.mult,
             edge=True)
        p.op("vector", "tensor_tensor", msq[:, :], smean[:, :], smean[:, :], ALU.mult)
        p.op("vector", "tensor_scalar", ssq[:, :], ssq[:, :], 1.0 / 1024, None, ALU.mult,
             edge=True)
        p.op("vector", "tensor_tensor", srstd[:, :], ssq[:, :], msq[:, :], ALU.subtract,
             inc=("sdv", 1))
        p.wait("scalar", "sdv", None)
        p.op("scalar", "activation", srstd[:, :], srstd[:, :], AF.Ln, bias=epsc[:, :],
             edge=True)
        p.op("scalar", "activation", srstd[:, :], srstd[:, :], AF.Exp, scale=-0.5,
             inc=("sac", 1))
        p.wait("vector", "sac", None)

    with p.block("dsetup"):
        for dst, srcd in ((id_f32, id_f32_d), (id_bf, id_bf_d), (epsc, eps_d)):
            p.op("sync", "dma_start", dst[:, :], srcd[:, :], inc=("sld", 16))
        p.op("gpsimd", "dma_start", projsb[:, :].rearrange("p (c e) -> p c e", e=16),
             proj_d.ap().rearrange("(c p) e -> p c e", p=128), inc=("sldg", 16))
        p.op("gpsimd", "dma_start", c1bf[:, :], c1_d[:, :], inc=("sldg", 16))
        p.op("gpsimd", "dma_start", c2bf[:, :], c2_d[:, :], inc=("sldg", 16))
        p.op("gpsimd", "dma_start", win1sb[:, :].rearrange("p (c r) -> p c r", r=512),
             win1_d[0:1024, :].rearrange("(c p) r -> p c r", p=128), inc=("sldg", 16))
        p.op("gpsimd", "dma_start", win2sb[:, :].rearrange("p (c r) -> p c r", r=512),
             win2_d[0:4096, :].rearrange("(c p) r -> p c r", p=128), inc=("sldg", 16))
        for wsrc, bdst in ((win1_d, bz1), (win2_d, bz2)):
            row = 1024 if wsrc is win1_d else 4096
            p.op("sync", "dma_start", wrow[:, :], wsrc[row:row + 1, :], inc=("sld", 16))
            p.wait("tensor", "sld", None)
            p.wait("tensor", "sdv", None)
            for k in range(4):
                p.op("tensor", "transpose", pZ[:, k:k + 1],
                     wrow[:, k * 128:(k + 1) * 128], id_f32[0:1, 0:1],
                     inc=("spe", 1) if k == 3 else None)
            p.wait("vector", "spe", None)
            p.op("vector", "tensor_copy", bdst[:, :], pZ[:, 0:4], inc=("sdv", 1))
            p.wait("sync", "sdv", None)
        for oc in range(32):
            p.wait("gpsimd", "sdv", None)
            p.op("gpsimd", "dma_start", wtmp[:, :],
                 wout1_d[oc * 128:(oc + 1) * 128, :], inc=("sldg", 16))
            p.wait("tensor", "sldg", None)
            p.wait("tensor", "sdv", None)
            for rc in range(4):
                p.op("tensor", "transpose", pT[:, rc * 128:(rc + 1) * 128],
                     wtmp[:, rc * 128:(rc + 1) * 128], id_bf[:, :],
                     inc=("spe", 1) if rc == 3 else None)
            p.wait("vector", "spe", None)
            p.op("vector", "tensor_copy",
                 woutT[:, :].rearrange("p (c n) -> p c n", n=4096)[:, :, oc * 128:(oc + 1) * 128],
                 pT[:, 0:512].rearrange("p (c n) -> p c n", n=128), inc=("sdv", 1))
        for oc in range(8):
            p.wait("gpsimd", "sdv", None)
            p.op("gpsimd", "dma_start", wtmp[:, :],
                 wout2_d[oc * 128:(oc + 1) * 128, :], inc=("sldg", 16))
            p.wait("tensor", "sldg", None)
            p.wait("tensor", "sdv", None)
            for rc in range(4):
                p.op("tensor", "transpose", pT[:, rc * 128:(rc + 1) * 128],
                     wtmp[:, rc * 128:(rc + 1) * 128], id_bf[:, :],
                     inc=("spe", 1) if rc == 3 else None)
            p.wait("vector", "spe", None)
            p.op("vector", "tensor_copy",
                 wout2T[:, :].rearrange("p (c n) -> p c n", n=1024)[:, :, oc * 128:(oc + 1) * 128],
                 pT[:, 0:512].rearrange("p (c n) -> p c n", n=128), inc=("sdv", 1))
        p.wait("tensor", "sdv", None)

    with p.block("d0"):
        for t in range(NTILE):
            xb = xbuf[t % 2]
            ab_ = abuf[t % 2]
            x3b = x3buf[t % 2]
            p.wait("sync", "sst", None)
            p.wait("sync", "sdv", None)
            p.op("sync", "dma_start", xb[:, :], x2_d[t * 128:(t + 1) * 128, :],
                 inc=("sld", 16))
            p.op("sync", "dma_start", ab_[:, :], ac_d[t * 128:(t + 1) * 128, :],
                 inc=("sld", 16))
            p.wait("scalar", "sld", None)
            p.wait("vector", "sld", None)
            emit_stats_d(ab_[:, :], scr[:, :])
            p.op("vector", "tensor_scalar", scr[:, :], ab_[:, :], smean[:, :], srstd[:, :],
                 ALU.subtract, ALU.mult, edge=True)
            p.op("vector", "tensor_tensor", x3b[:, :], scr[:, :], xb[:, :], ALU.add,
                 inc=("sdv", 1))
            p.wait("sync", "sdv", None)
            p.op("sync", "dma_start", x3_dram[t * 128:(t + 1) * 128, :], x3b[:, :],
                 inc=("sst", 16))
            p.wait("scalar", "sdv", None)
            emit_stats_d(x3b[:, :], scr[:, :])
            p.op("vector", "tensor_scalar", xln_bf[:, :], x3b[:, :], smean[:, :],
                 srstd[:, :], ALU.subtract, ALU.mult, inc=("sdv", 1))
            p.wait("tensor", "sdv", None)
            p.wait("tensor", "sac", None)
            for half in range(2):
                for c4 in range(4):
                    c = half * 4 + c4
                    p.op("tensor", "transpose",
                         pT[:, half * 512 + c4 * 128: half * 512 + (c4 + 1) * 128],
                         xln_bf[:, c * 128:(c + 1) * 128], id_bf[:, :],
                         inc=("spe", 1) if c4 == 3 else None)
                p.wait("vector", "spe", None)
                p.op("vector", "tensor_copy",
                     xlnT[:, :].rearrange("p (c n) -> p c n", n=NT)
                         [:, half * 4:(half + 1) * 4, t * 128:(t + 1) * 128],
                     pT[:, half * 512:(half + 1) * 512].rearrange("p (c n) -> p c n", n=128),
                     inc=("sdv", 1))
                p.wait("tensor", "sdv", None)

    with p.block("d2"):
        for t in range(NTILE):
            p.wait("tensor", "sac", None)
            p.wait("tensor", "sdv", None)
            for c in range(8):
                p.op("tensor", "matmul", pZ[:, 0:16],
                     xlnT[:, :].rearrange("p (c n) -> p c n", n=NT)[:, c, t * 128:(t + 1) * 128],
                     projsb[:, c * 16:(c + 1) * 16], start=(c == 0), stop=(c == 7),
                     inc=("spe", 1) if c == 7 else None)
            p.wait("vector", "spe", None)
            p.op("vector", "tensor_reduce", ssum[:, :], pZ[:, 0:16], AX.X, ALU.add,
                 inc=("sdv", 1))
            p.wait("scalar", "spe", None)
            p.op("scalar", "activation", dmp16[:, :], pZ[:, 0:16], AF.Square,
                 accum_out=ssq[:, :], inc=("sac", 1))
            p.wait("vector", "sac", None)
            p.op("vector", "tensor_scalar", smean[:, :], ssum[:, :], 1.0 / 16, None,
                 ALU.mult, edge=True)
            p.op("vector", "tensor_tensor", msq[:, :], smean[:, :], smean[:, :], ALU.mult)
            p.op("vector", "tensor_scalar", ssq[:, :], ssq[:, :], 1.0 / 16, None, ALU.mult,
                 edge=True)
            p.op("vector", "tensor_tensor", srstd[:, :], ssq[:, :], msq[:, :], ALU.subtract,
                 inc=("sdv", 1))
            p.wait("scalar", "sdv", None)
            p.op("scalar", "activation", srstd[:, :], srstd[:, :], AF.Ln, bias=epsc[:, :],
                 edge=True)
            p.op("scalar", "activation", srstd[:, :], srstd[:, :], AF.Exp, scale=-0.5,
                 inc=("sac", 1))
            p.wait("vector", "sac", None)
            p.op("vector", "tensor_scalar", srstd[:, :], srstd[:, :], 0.5, None, ALU.mult,
                 edge=True)
            p.op("vector", "tensor_scalar", zt[:, :], pZ[:, 0:16], smean[:, :], srstd[:, :],
                 ALU.subtract, ALU.mult, edge=True)
            p.op("vector", "tensor_reduce", nmax[:, :], zt[:, :], AX.X, ALU.max,
                 negate=True, edge=True)
            p.op("vector", "tensor_scalar", zt[:, :], zt[:, :], nmax[:, :], None, ALU.add,
                 edge=True)
            p.op("vector", "memset", losA[:, :], -1.0)
            p.op("vector", "memset", hisA[:, :], 0.0, inc=("sdv", 1))
            p.wait("vector", "sdv", None)
            lo, lo2, hi, hi2 = losA, losB, hisA, hisB
            for it in range(NBIS):
                p.op("vector", "tensor_tensor", mid[:, :], lo[:, :], hi[:, :], ALU.add,
                     edge=True)
                p.op("vector", "tensor_scalar", mid[:, :], mid[:, :], 0.5, None, ALU.mult,
                     edge=True)
                p.op("vector", "tensor_scalar", rl[:, :], zt[:, :], mid[:, :], 0.0,
                     ALU.subtract, ALU.max, edge=True)
                p.op("vector", "scalar_tensor_tensor", dmp16[:, :], rl[:, :], 1.0, rl[:, :],
                     ALU.bypass, ALU.mult, accum_out=gac[:, :], edge=True)
                p.op("vector", "tensor_scalar", mk_[:, :], gac[:, :], 1.0, None, ALU.is_gt,
                     edge=True)
                p.op("vector", "tensor_tensor", dd[:, :], mid[:, :], lo[:, :], ALU.subtract,
                     edge=True)
                p.op("vector", "scalar_tensor_tensor", lo2[:, :], dd[:, :], mk_[:, :],
                     lo[:, :], ALU.mult, ALU.add, edge=True)
                p.op("vector", "tensor_tensor", dd[:, :], hi[:, :], mid[:, :], ALU.subtract,
                     edge=True)
                p.op("vector", "scalar_tensor_tensor", hi2[:, :], dd[:, :], mk_[:, :],
                     mid[:, :], ALU.mult, ALU.add, edge=True)
                lo, lo2 = lo2, lo
                hi, hi2 = hi2, hi
            tau, tau2 = lo, lo2
            for it in range(NNEWT):
                p.op("vector", "tensor_scalar", rl[:, :], zt[:, :], tau[:, :], 0.0,
                     ALU.subtract, ALU.max, edge=True)
                p.op("vector", "scalar_tensor_tensor", dmp16[:, :], rl[:, :], 1.0, rl[:, :],
                     ALU.bypass, ALU.mult, accum_out=gac[:, :], edge=True)
                p.op("vector", "tensor_scalar", dmp16[:, :], rl[:, :], 1.0, 0.0, ALU.mult,
                     ALU.add, accum_out=s1c[:, :], edge=True)
                p.op("vector", "reciprocal", rec[:, :], s1c[:, :], edge=True)
                p.op("vector", "tensor_scalar", gac[:, :], gac[:, :], 1.0, None,
                     ALU.subtract, edge=True)
                p.op("vector", "tensor_tensor", dd[:, :], gac[:, :], rec[:, :], ALU.mult,
                     edge=True)
                p.op("vector", "tensor_scalar", dd[:, :], dd[:, :], 0.5, None, ALU.mult,
                     edge=True)
                p.op("vector", "tensor_tensor", tau2[:, :], tau[:, :], dd[:, :], ALU.add,
                     edge=True)
                tau, tau2 = tau2, tau
            p.op("vector", "tensor_scalar", rl[:, :], zt[:, :], tau[:, :], 0.0,
                 ALU.subtract, ALU.max, edge=True)
            p.op("vector", "tensor_tensor", dmp16[:, :], rl[:, :], rl[:, :], ALU.mult,
                 edge=True)
            p.op("vector", "tensor_copy", ab[:, :], dmp16[:, :], inc=("sdv", 1))
            p.wait("tensor", "sdv", None)
            p.op("tensor", "transpose", pT[0:16, 0:128], ab[:, :], id_bf[:, :],
                 inc=("spe", 1))
            p.wait("vector", "spe", None)
            p.op("vector", "tensor_copy", aT[:, t * 128:(t + 1) * 128], pT[0:16, 0:128],
                 inc=("sdv", 1))
            p.wait("tensor", "sdv", None)

    with p.block("dmain"):
        for g in range(NG):
            for gi, (cbf, gT) in enumerate(((c1bf, g1T), (c2bf, g2T))):
                for rc in range(4):
                    p.wait("tensor", "sac", None)
                    p.wait("tensor", "sdv", None)
                    p.op("tensor", "matmul", pZ[:, 0:512], cbf[:, rc * 128:(rc + 1) * 128],
                         aT[:, g * 512:(g + 1) * 512],
                         start=True, stop=True, inc=("spe", 1))
                    p.wait("scalar", "spe", None)
                    p.op("scalar", "activation",
                         gT[:, rc * 512:(rc + 1) * 512],
                         pZ[:, 0:512], AF.Copy, bias=0.0, inc=("sac", 1))
            for tt in range(1):
                tok0 = g * 512
                for rc in range(4):
                    p.wait("tensor", "sac", None)
                    p.wait("tensor", "sdv", None)
                    for dc in range(8):
                        p.op("tensor", "matmul", pZ[:, 0:512],
                             win1sb[:, dc * 512 + rc * 128: dc * 512 + (rc + 1) * 128],
                             xlnT[:, :].rearrange("p (c n) -> p c n", n=NT)[:, dc, tok0:tok0 + 512],
                             start=(dc == 0), stop=(dc == 7),
                             inc=("spe", 1) if dc == 7 else None)
                    p.wait("vector", "spe", None)
                    p.op("vector", "scalar_tensor_tensor", zg1[:, rc * 512:(rc + 1) * 512],
                         pZ[:, 0:512], bz1[:, rc:rc + 1],
                         g1T[:, rc * 512:(rc + 1) * 512],
                         ALU.add, ALU.mult, inc=("sdv", 1))
                p.wait("tensor", "sdv", None)
                for oc in range(32):
                    ph = pH2[:, (oc % 2) * 512:(oc % 2 + 1) * 512]
                    p.wait("tensor", "sdv", None)
                    for rc in range(4):
                        p.op("tensor", "matmul", ph,
                             woutT[:, rc * 4096 + oc * 128: rc * 4096 + (oc + 1) * 128],
                             zg1[:, rc * 512:(rc + 1) * 512],
                             start=(rc == 0), stop=(rc == 3),
                             inc=("spe", 1) if rc == 3 else None)
                    p.wait("scalar", "spe", None)
                    p.op("scalar", "activation", hb[:, :], ph, AF.Gelu, bias=0.0,
                         inc=("sac", 1))
                    p.wait("tensor", "sac", None)
                    for r2c in range(4):
                        pp = pA if r2c < 2 else pB
                        p.op("tensor", "matmul", pp[:, (r2c % 2) * 512:(r2c % 2 + 1) * 512],
                             win2sb[:, oc * 512 + r2c * 128: oc * 512 + (r2c + 1) * 128],
                             hb[:, :], start=(oc == 0), stop=(oc == 31),
                             inc=("spe", 1) if (oc == 31 and r2c == 3) else None)
                p.wait("vector", "spe", None)
                for r2c in range(4):
                    pp = pA if r2c < 2 else pB
                    p.op("vector", "scalar_tensor_tensor", zg2[:, r2c * 512:(r2c + 1) * 512],
                         pp[:, (r2c % 2) * 512:(r2c % 2 + 1) * 512], bz2[:, r2c:r2c + 1],
                         g2T[:, r2c * 512:(r2c + 1) * 512],
                         ALU.add, ALU.mult, inc=("sdv", 1) if r2c == 3 else None)
                p.wait("tensor", "sdv", None)
                for t4 in range(4):
                    gt = (tok0 // 128) + t4
                    x3b = x3buf[t4 % 2]
                    ob = obuf[t4 % 2]
                    p.wait("sync", "sst", None)
                    p.op("sync", "dma_start", x3b[:, :],
                         x3_dram[gt * 128:(gt + 1) * 128, :], inc=("sld", 16))
                    p.wait("tensor", "sac", None)
                    p.wait("tensor", "sdv", None)
                    for half in range(2):
                        for r2c in range(4):
                            p.op("tensor", "matmul",
                                 pH2[:, half * 512:(half + 1) * 512],
                                 zg2[:, r2c * 512 + t4 * 128: r2c * 512 + (t4 + 1) * 128],
                                 wout2T[:, r2c * 1024 + half * 512: r2c * 1024 + (half + 1) * 512],
                                 start=(r2c == 0), stop=(r2c == 3),
                                 inc=("spe", 1) if (half == 1 and r2c == 3) else None)
                    p.wait("scalar", "spe", None)
                    p.wait("scalar", "sld", None)
                    emit_stats_d(pH2[:, :], scr[:, :])
                    p.op("vector", "tensor_scalar", scr[:, :], pH2[:, :], smean[:, :],
                         srstd[:, :], ALU.subtract, ALU.mult, edge=True)
                    p.op("vector", "tensor_tensor", ob[:, :], scr[:, :], x3b[:, :], ALU.add,
                         inc=("sdv", 1))
                    p.wait("sync", "sdv", None)
                    p.op("sync", "dma_start", o[gt * 128:(gt + 1) * 128, :], ob[:, :],
                         inc=("sst", 16))
        p.wait("sync", "sst", None)
    stD.close()

    st.close()
    return nc


def inputs_full(x_shard, w):
    m = dict(make_consts_a())
    m["x"] = np.ascontiguousarray(x_shard, dtype=np.float32)
    m["wq"] = w["L0_attn_Wq"]; m["wk"] = w["L0_attn_Wk"]
    m["wv"] = w["L0_attn_Wv"]; m["wp"] = w["L0_attn_Wproj"]
    m["qrw"] = w["L0_attn_q_rms_w"].reshape(1, 64)
    m["krw"] = w["L0_attn_k_rms_w"].reshape(1, 64)
    m["omg"] = w["L0_attn_omega"].reshape(16, 1)
    m["u"] = w["L0_attn_u"].reshape(16, 64)
    m["rw"] = w["L0_ffn_router_W"]
    m["rb"] = np.ascontiguousarray(w["L0_ffn_router_b"]).reshape(1, 8)
    m["w1"] = w["L0_ffn_W1"]; m["b1"] = w["L0_ffn_b1"]
    m["w2"] = w["L0_ffn_W2"]; m["b2"] = w["L0_ffn_b2"]
    m["wq1"] = w["L1_attn_Wq"]; m["wk1"] = w["L1_attn_Wk"]; m["wv1"] = w["L1_attn_Wv"]
    m["proj"] = w["L1_ffn_proj"]
    m["c1"] = w["L1_ffn_C1"]; m["c2"] = w["L1_ffn_C2"]
    m["win1"] = w["L1_ffn_Win1"]; m["wout1"] = w["L1_ffn_Wout1"]
    m["win2"] = w["L1_ffn_Win2"]; m["wout2"] = w["L1_ffn_Wout2"]
    m["cones_bf"] = np.ones((128, 128), dtype=ml_dtypes.bfloat16)
    return m



def kernel(**inputs):
    w = inputs
    x = np.ascontiguousarray(w["L0_attn_x"], dtype=np.float32).reshape(16, 2048, 1024)
    ncores = 8
    nc = build_full(ncores, 2, 16)
    in_maps = [inputs_full(x[2 * c:2 * c + 2].reshape(4096, 1024), w)
               for c in range(ncores)]
    res = run_bass_kernel_spmd(nc, in_maps, core_ids=list(range(ncores)))
    out = np.stack([np.asarray(r["out"]).reshape(2, 2048, 1024) for r in res.results]
                   ).reshape(16, 2048, 1024)
    return out.astype(np.float32)
